# version 01
# baseline (speedup 1.0000x reference)
import contextlib
import numpy as np
import ml_dtypes
import concourse.bass as bass
import concourse.mybir as mybir
from concourse.bass_utils import run_bass_kernel_spmd

F32 = mybir.dt.float32
BF16 = mybir.dt.bfloat16
I32 = mybir.dt.int32
AF = mybir.ActivationFunctionType
ALU = mybir.AluOpType
AX = mybir.AxisListType

B, T, D = 4, 8192, 1024
BF = ml_dtypes.bfloat16

ENGS = ("sync", "gpsimd", "vector", "scalar", "tensor")
DMA_ENGS = ("sync", "gpsimd", "scalar")


class Op:
    __slots__ = ("eng", "fn", "deps", "inc", "need_inc", "sem_val", "sem_key",
                 "stream_prev")

    def __init__(self, eng, fn, deps, inc):
        self.eng = eng
        self.fn = fn
        self.deps = [d for d in deps if d is not None]
        self.inc = inc
        self.need_inc = False
        self.sem_val = None


class KB:

    def __init__(self, nc, ctx):
        self.nc = nc
        self.ctx = ctx
        self.plan = {e: [] for e in ENGS}
        self._sb_names = 0

    def shared_sb(self, key, shape, dtype):
        if not hasattr(self, "_shcache"):
            self._shcache = {}
        k = (key, tuple(shape), dtype)
        if k not in self._shcache:
            self._shcache[k] = self.sb(shape, dtype, key)
        return self._shcache[k]

    def sb(self, shape, dtype, name=None):
        self._sb_names += 1
        return self.ctx.enter_context(
            self.nc.sbuf_tensor(f"{name or 'sb'}_{self._sb_names}", shape, dtype))

    def ps(self, shape, dtype=F32, name=None):
        self._sb_names += 1
        return self.ctx.enter_context(
            self.nc.psum_tensor(f"{name or 'ps'}_{self._sb_names}", shape, dtype))

    def op(self, eng, fn, deps=(), inc=1):
        o = Op(eng, fn, deps, inc)
        self.plan[eng].append(o)
        return o

    def dma(self, eng, out, in_, deps=(), **kw):
        assert eng in DMA_ENGS
        return self.op(eng, lambda q: q.dma_start(out=out, in_=in_, **kw), deps, inc=16)

    def mm(self, lhsT, rhs, out, start, stop, deps=()):
        return self.op("tensor",
                       lambda q: q.matmul(out, lhsT=lhsT, rhs=rhs, start=start, stop=stop),
                       deps)

    def transp(self, out_ps, in_sb, ident, deps=()):
        return self.op("tensor", lambda q: q.transpose(out_ps, in_=in_sb, identity=ident), deps)

    def act(self, out, in_, func, deps=(), bias=0.0, scale=1.0, accum_out=None):
        return self.op("scalar",
                       lambda q: q.activation(out, in_, func, bias=bias, scale=scale,
                                              accum_out=accum_out), deps)

    def copy(self, eng, out, in_, deps=()):
        if eng == "scalar":
            return self.op(eng, lambda q: q.copy(out=out, in_=in_), deps)
        return self.op(eng, lambda q: q.tensor_copy(out=out, in_=in_), deps)

    def tt(self, eng, out, in0, in1, alu, deps=()):
        return self.op(eng, lambda q: q.tensor_tensor(out=out, in0=in0, in1=in1, op=alu), deps)

    def tsc(self, eng, out, in0, s1, op0, s2=None, op1=None, deps=(), accum_out=None):
        if op1 is None:
            return self.op(eng,
                           lambda q: q.tensor_scalar(out=out, in0=in0, scalar1=s1, scalar2=None,
                                                     op0=op0, accum_out=accum_out), deps)
        return self.op(eng,
                       lambda q: q.tensor_scalar(out=out, in0=in0, scalar1=s1, scalar2=s2,
                                                 op0=op0, op1=op1, accum_out=accum_out), deps)

    def stt(self, eng, out, in0, scalar, in1, op0, op1, deps=()):
        return self.op(eng,
                       lambda q: q.scalar_tensor_tensor(out=out, in0=in0, scalar=scalar,
                                                        in1=in1, op0=op0, op1=op1), deps)

    def red(self, out, in_, alu, deps=(), negate=False):
        return self.op("vector",
                       lambda q: q.tensor_reduce(out=out, in_=in_, axis=AX.X, op=alu,
                                                 negate=negate), deps)

    def scan(self, out, d0, d1, init, op0, op1, deps=()):
        return self.op("vector",
                       lambda q: q.tensor_tensor_scan(out=out, data0=d0, data1=d1,
                                                      initial=init, op0=op0, op1=op1), deps)

    def memset(self, eng, ap, val, deps=()):
        return self.op(eng, lambda q: q.memset(ap, val), deps)

    def iota(self, ap, pattern, base=0, chmul=0, deps=()):
        return self.op("gpsimd",
                       lambda q: q.iota(ap, pattern, base=base, channel_multiplier=chmul,
                                        allow_small_or_imprecise_dtypes=True), deps)

    def igather(self, out, in_, idx_ap, deps=(), axis=0, element_offset=0):
        return self.op("gpsimd",
                       lambda q: q.indirect_dma_start(
                           out=out, out_offset=None, in_=in_,
                           in_offset=bass.IndirectOffsetOnAxis(ap=idx_ap, axis=axis),
                           element_offset=element_offset), deps, inc=16)

    def iscatter(self, out, idx_ap, in_, deps=(), axis=0, element_offset=0):
        return self.op("gpsimd",
                       lambda q: q.indirect_dma_start(
                           out=out, out_offset=bass.IndirectOffsetOnAxis(ap=idx_ap, axis=axis),
                           in_=in_, in_offset=None,
                           element_offset=element_offset), deps, inc=16)

    KD = 6

    def finalize(self, final_deps):
        nc = self.nc
        stream_prev = {}
        dma_idx = {e: 0 for e in DMA_ENGS}
        for e in ENGS:
            for o in self.plan[e]:
                if o.inc == 16:
                    j = dma_idx[e] % self.KD
                    dma_idx[e] += 1
                    o.sem_key = (e, "d", j)
                    o.need_inc = True
                    o.stream_prev = stream_prev.get(o.sem_key)
                    stream_prev[o.sem_key] = o
                else:
                    o.sem_key = (e, "c")
                    o.stream_prev = None
                for d in o.deps:
                    d.need_inc = True
        for d in final_deps:
            d.need_inc = True
        cum = {}
        for e in ENGS:
            for o in self.plan[e]:
                if o.need_inc:
                    cum[o.sem_key] = cum.get(o.sem_key, 0) + o.inc
                    o.sem_val = cum[o.sem_key]
        semh = {k: self.ctx.enter_context(nc.semaphore(f"s_{k[0]}_{k[1]}{k[2] if len(k)>2 else ''}"))
                for k in cum}
        block = self.ctx.enter_context(nc.Block())
        plan = self.plan
        fin = list(final_deps)

        def run_engine(q, e):
            waited = {}
            def wait_for(d):
                v = waited.get(d.sem_key, 0)
                if v < d.sem_val:
                    q.wait_ge(semh[d.sem_key], d.sem_val)
                    waited[d.sem_key] = d.sem_val
            for o in plan[e]:
                for d in o.deps:
                    assert d.sem_val is not None, "dep has no sem value"
                    wait_for(d)
                if o.inc == 16 and o.stream_prev is not None:
                    wait_for(o.stream_prev)
                inst = o.fn(q)
                if o.need_inc and inst is not None:
                    inst.then_inc(semh[o.sem_key], o.inc)
            if e == "sync":
                for d in fin:
                    wait_for(d)

        @block.sync
        def _(q):
            run_engine(q, "sync")

        @block.gpsimd
        def _(q):
            run_engine(q, "gpsimd")

        @block.vector
        def _(q):
            run_engine(q, "vector")

        @block.scalar
        def _(q):
            run_engine(q, "scalar")

        @block.tensor
        def _(q):
            run_engine(q, "tensor")


def ceil_div(a, b):
    return (a + b - 1) // b


class Consts:

    def __init__(self, kb):
        nc = kb.nc
        self.ones_bf = kb.sb([128, 128], BF16, "c_ones_bf")
        self.ident_bf = kb.sb([128, 128], BF16, "c_ident_bf")
        ident_dram = nc.dram_tensor("c_ident_dram", [128, 128], BF16,
                                    kind="ExternalInput").ap()
        l1 = kb.dma("sync", self.ident_bf[:], ident_dram)
        l2 = kb.memset("vector", self.ones_bf[:], 1.0)
        self.tok = (l1, l2)

    @staticmethod
    def host_inputs():
        return {"c_ident_dram": np.eye(128, dtype=BF)}


def emit_ln(kb, cn, x_dram, xn_dram, t_lo, t_hi, psum_ab, w_ap=None, b_ap=None, wb_deps=(),
            eps=1e-5, rms=False, x_deps=(), psum_deps=(), xn32_dram=None):
    TC = 1024
    xb = [kb.shared_sb(f"ln_xb{i}", [128, TC], BF16) for i in range(8)]
    sq = kb.shared_sb("ln_sq", [128, 512], BF16)
    pa, pb = psum_ab
    mu = kb.shared_sb("ln_mu", [128, 512], F32)
    var = kb.shared_sb("ln_var", [128, 512], F32)
    tmp = kb.shared_sb("ln_tmp", [128, 512], F32)
    coefa = kb.shared_sb("ln_ca", [128, TC], F32)
    coefb = kb.shared_sb("ln_cb", [128, TC], F32)
    xn = kb.shared_sb("ln_xn", [128, TC], BF16)
    xn32 = kb.shared_sb("ln_xn32", [128, TC], F32) if xn32_dram is not None else None
    xf32 = kb.shared_sb("ln_xf32", [128, TC], F32) if xn32_dram is not None else None
    prev_store = None
    prev_use_xb = [None] * 8
    outs = []

    for t0 in range(t_lo, t_hi, TC):
        loads = []
        for i in range(8):
            loads.append(kb.dma("gpsimd", xb[i][:], x_dram[128 * i:128 * (i + 1), t0:t0 + TC],
                                deps=(prev_use_xb[i],) + tuple(x_deps)))
        stat_done = []
        prev_o1 = None
        prev_sq = None
        for g in range(TC // 512):
            gsl = slice(g * 512, (g + 1) * 512)
            pa_mm = None
            pb_mm = None
            for i in range(8):
                sq_op = kb.act(sq[:], xb[i][:, gsl], AF.Square,
                               deps=(loads[i], prev_sq))
                pa_mm = kb.mm(cn.ones_bf[:], xb[i][:, gsl], pa[:], start=(i == 0), stop=(i == 7),
                              deps=(loads[i], cn.tok[1]) + tuple(psum_deps) + ((prev_o1,) if i == 0 else ()))
                pb_mm = kb.mm(cn.ones_bf[:], sq[:], pb[:], start=(i == 0), stop=(i == 7),
                              deps=(sq_op,))
                prev_sq = pb_mm
            if rms:
                o1 = kb.tsc("vector", var[:], pb[:], 1.0 / 1024, ALU.mult, eps, ALU.add,
                            deps=(pb_mm,))
                o2 = kb.act(var[:], var[:], AF.Ln, deps=(o1,))
                o3 = kb.act(coefa[:, gsl], var[:], AF.Exp, scale=-0.5, deps=(o2,))
                o4 = kb.memset("vector", coefb[:, gsl], 0.0)
                stat_done += [o3, o4]
                prev_o1 = o1
            else:
                o1 = kb.tsc("vector", mu[:], pa[:], 1.0 / 1024, ALU.mult, deps=(pa_mm,))
                o2 = kb.tsc("vector", var[:], pb[:], 1.0 / 1024, ALU.mult, deps=(pb_mm,))
                o3 = kb.tt("vector", tmp[:], mu[:], mu[:], ALU.mult, deps=(o1,))
                o4 = kb.tt("vector", var[:], var[:], tmp[:], ALU.subtract, deps=(o2, o3))
                o5 = kb.tsc("vector", var[:], var[:], eps, ALU.add, deps=(o4,))
                o6 = kb.act(var[:], var[:], AF.Ln, deps=(o5,))
                o7 = kb.act(coefa[:, gsl], var[:], AF.Exp, scale=-0.5, deps=(o6,))
                o8 = kb.tt("vector", coefb[:, gsl], mu[:], coefa[:, gsl], ALU.mult,
                           deps=(o7, o1))
                o9 = kb.tsc("vector", coefb[:, gsl], coefb[:, gsl], -1.0, ALU.mult, deps=(o8,))
                stat_done += [o7, o9]
                prev_o1 = o9
        for i in range(8):
            if xn32_dram is not None:
                lf = kb.dma("gpsimd", xf32[:], x_dram[128 * i:128 * (i + 1), t0:t0 + TC],
                            deps=(prev_store,) + tuple(x_deps))
                m1 = kb.tt("gpsimd", xn32[:], xf32[:], coefa[:], ALU.mult,
                           deps=tuple(stat_done) + (lf, prev_store))
                m2 = kb.tt("gpsimd", xn32[:], xn32[:], coefb[:], ALU.add, deps=(m1,))
                last = m2
                if w_ap is not None:
                    last = kb.tsc("gpsimd", xn32[:], xn32[:], w_ap(i), ALU.mult,
                                  b_ap(i) if b_ap is not None else None,
                                  ALU.add if b_ap is not None else None,
                                  deps=(m2,) + tuple(wb_deps))
                cbf = kb.copy("gpsimd", xn[:], xn32[:], deps=(last,))
                st1 = kb.dma("scalar", xn_dram[128 * i:128 * (i + 1), t0:t0 + TC], xn[:],
                             deps=(cbf,))
                st = kb.dma("scalar", xn32_dram[128 * i:128 * (i + 1), t0:t0 + TC],
                            xn32[:], deps=(last, st1))
                prev_use_xb[i] = m1
                prev_store = st
                outs.append(st)
                continue
            m1 = kb.tt("gpsimd", xn[:], xb[i][:], coefa[:], ALU.mult,
                       deps=tuple(stat_done) + (loads[i], prev_store))
            m2 = kb.tt("gpsimd", xn[:], xn[:], coefb[:], ALU.add, deps=(m1,))
            last = m2
            if w_ap is not None:
                if b_ap is not None:
                    last = kb.tsc("gpsimd", xn[:], xn[:], w_ap(i), ALU.mult, b_ap(i), ALU.add,
                                  deps=(m2,) + tuple(wb_deps))
                else:
                    last = kb.tsc("gpsimd", xn[:], xn[:], w_ap(i), ALU.mult,
                                  deps=(m2,) + tuple(wb_deps))
            st = kb.dma("scalar", xn_dram[128 * i:128 * (i + 1), t0:t0 + TC], xn[:],
                        deps=(last,))
            prev_use_xb[i] = m1
            prev_store = st
            outs.append(st)
    return outs



def emit_proj(kb, wsrcs, rhs_dram, out_cb, Kdim, Mdim, t_lo, t_hi,
              w_deps=(), rhs_deps=(), NT=512, wbufs=None, psums=None, ps_state=None,
              shared=None, name=""):
    nW = len(wsrcs)
    nK = Kdim // 128
    nM = Mdim // 128
    w_tiles = []
    w_toks = []
    for wi, ws in enumerate(wsrcs):
        if wbufs is not None and wbufs[wi] is not None:
            tile, lastuse = wbufs[wi]
        else:
            tile, lastuse = kb.sb([128, nK * Mdim], BF16, f"w_{name}{wi}"), None
        wl = []
        for k in range(nK):
            wl.append(kb.dma("gpsimd", tile[:, k * Mdim:(k + 1) * Mdim],
                             ws[128 * k:128 * (k + 1), :],
                             deps=tuple(w_deps) + ((lastuse,) if lastuse else ())))
        w_tiles.append(tile)
        w_toks.append(tuple(wl))
    if psums is None:
        psums = [[kb.ps([128, NT], F32) for _ in range(2)] for _ in range(nW)]
    if shared is not None:
        key = ("xb", NT)
        if key not in shared or len(shared[key][0]) < nK:
            shared[key] = ([kb.sb([128, NT], BF16, "xsh") for k in range(nK)], None)
        xb, xb_last = shared[key]
    else:
        xb, xb_last = [kb.sb([128, NT], BF16, "x_" + name) for k in range(nK)], None
    ps_free = ps_state if ps_state is not None else {}
    prev_use_xb = xb_last
    pi = 0
    final_ops = []
    last_mm = None
    for t0 in range(t_lo, t_hi, NT):
        loads = []
        for k in range(nK):
            loads.append(kb.dma("sync", xb[k][:], rhs_dram[128 * k:128 * (k + 1), t0:t0 + NT],
                                deps=((prev_use_xb,) if prev_use_xb else ()) + tuple(rhs_deps)))
        for mi in range(nM):
            ps_here = []
            lasts = []
            for wi in range(nW):
                ps = psums[wi][pi % len(psums[wi])]
                last = None
                for k in range(nK):
                    last = kb.mm(w_tiles[wi][:, k * Mdim + mi * 128: k * Mdim + mi * 128 + 128],
                                 xb[k][:], ps[:], start=(k == 0), stop=(k == nK - 1),
                                 deps=(w_toks[wi][k], loads[k]) +
                                      ((ps_free.get(id(ps)),) if k == 0 else ()))
                ps_here.append(ps)
                lasts.append(last)
            last_mm = lasts[-1]
            fo = out_cb(mi, t0, ps_here, tuple(lasts))
            for ps in ps_here:
                ps_free[id(ps)] = fo
            final_ops.append(fo)
            pi += 1
        prev_use_xb = last_mm
    w_last = last_mm
    if shared is not None:
        shared[("xb", NT)] = (xb, last_mm)
    return final_ops, [(w_tiles[i], w_last) for i in range(nW)], prev_use_xb



def build_s2_dense():
    nc = bass.Bass("TRN2", target_bir_lowering=False)
    x = nc.dram_tensor("x", [D, T], F32, kind="ExternalInput").ap()
    ew1t = nc.dram_tensor("ew1t", [6, D, 1024], F32, kind="ExternalInput").ap()
    ew2t = nc.dram_tensor("ew2t", [6, 1024, D], F32, kind="ExternalInput").ap()
    ew3t = nc.dram_tensor("ew3t", [6, D, 1024], F32, kind="ExternalInput").ap()
    fc1t = nc.dram_tensor("fc1t", [D, 1024], F32, kind="ExternalInput").ap()
    fc2t = nc.dram_tensor("fc2t", [D, 1024], F32, kind="ExternalInput").ap()
    fc3t = nc.dram_tensor("fc3t", [1024, D], F32, kind="ExternalInput").ap()
    gate_wt = nc.dram_tensor("gate_wt", [D, 6], F32, kind="ExternalInput").ap()
    gate_b = nc.dram_tensor("gate_b", [1, 6], F32, kind="ExternalInput").ap()
    xo = nc.dram_tensor("xo", [D, T], F32, kind="ExternalOutput").ap()

    xn_dram = nc.dram_tensor("xn_scr", [D, T], BF16, kind="Internal").ap()
    hh_dram = nc.dram_tensor("hh_scr", [1024, T], BF16, kind="Internal").ap()
    acc_dram = nc.dram_tensor("acc_scr", [D, T], F32, kind="Internal").ap()
    ge6_dram = nc.dram_tensor("ge6_scr", [6, T], F32, kind="Internal").ap()

    with contextlib.ExitStack() as ctx:
        kb = KB(nc, ctx)
        cn = Consts(kb)
        pool = [kb.ps([128, 512], F32, f"pool{i}") for i in range(8)]
        kb.pool = pool

        ln_out = emit_ln(kb, cn, x, xn_dram, 0, T, (pool[0], pool[1]))
        ln_tok = (ln_out[-1],)

        ge_last = emit_l0_router(kb, cn, nc, xn_dram, gate_wt, gate_b, ge6_dram, ln_tok,
                                 pool[6], pool[7])

        h1sb = kb.sb([128, 512], BF16)
        hhsb = kb.sb([128, 512], BF16)
        ysb = kb.sb([128, 512], F32)
        gebt = kb.sb([128, 512], F32)
        wb13 = [None, None]
        wb2 = [None]
        ps13 = [[pool[0], pool[1]], [pool[2], pool[3]]]
        ps2 = [[pool[4], pool[5]]]
        ps_state = {id(pool[0]): ln_tok[0], id(pool[1]): ln_tok[0]}
        shr = {}
        prev_hh_store = {}
        prev_acc = [None]

        war = {"h1_rd": None, "hh_st": None, "ge_rd": None}

        def gated(e, first):
            def cb13(mi, t0, ps_list, deps):
                o1 = kb.act(h1sb[:], ps_list[0][:], AF.Silu,
                            deps=deps + (war["h1_rd"],))
                o2 = kb.tt("vector", hhsb[:], ps_list[1][:], h1sb[:], ALU.mult,
                           deps=deps + (o1, war["hh_st"]))
                s = kb.dma("scalar", hh_dram[128 * mi:128 * (mi + 1), t0:t0 + 512], hhsb[:],
                           deps=(o2,))
                war["h1_rd"] = o2
                war["hh_st"] = s
                prev_hh_store[(mi, t0)] = s
                return s
            _, wl, _ = emit_proj(kb, [ew1t[e], ew3t[e]], xn_dram, cb13, D, 1024, 0, T,
                                 rhs_deps=ln_tok, wbufs=wb13, psums=ps13, ps_state=ps_state,
                                 shared=shr, name="e13")
            wb13[0], wb13[1] = wl[0], wl[1]

            def cb2(mi, t0, ps_list, deps):
                ld = kb.dma("sync", gebt[:],
                            ge6_dram[e:e + 1, t0:t0 + 512].to_broadcast([128, 512]),
                            deps=deps + (ge_last, war["ge_rd"]))
                o1 = kb.tt("vector", ysb[:], ps_list[0][:], gebt[:], ALU.mult,
                           deps=(ld,) + deps + (prev_acc[0],))
                war["ge_rd"] = o1
                s = kb.dma("gpsimd", acc_dram[128 * mi:128 * (mi + 1), t0:t0 + 512], ysb[:],
                           deps=(o1,), **({} if first else dict(accum_op=ALU.add)))
                prev_acc[0] = s
                return s
            _, wl2, _ = emit_proj(kb, [ew2t[e]], hh_dram, cb2, 1024, D, 0, T,
                                  rhs_deps=(prev_hh_store[(7, T - 512)],),
                                  wbufs=wb2, psums=ps2, ps_state=ps_state, shared=shr, name="e2")
            wb2[0] = wl2[0]

        for e in range(6):
            gated(e, first=(e == 0))

        def cbs13(mi, t0, ps_list, deps):
            o1 = kb.act(h1sb[:], ps_list[0][:], AF.Silu,
                        deps=deps + (war["h1_rd"], prev_hh_store[(mi, t0)]))
            o2 = kb.tt("vector", hhsb[:], ps_list[1][:], h1sb[:], ALU.mult,
                       deps=deps + (o1, war["hh_st"]))
            s = kb.dma("scalar", hh_dram[128 * mi:128 * (mi + 1), t0:t0 + 512], hhsb[:],
                       deps=(o2,))
            war["h1_rd"] = o2
            war["hh_st"] = s
            prev_hh_store[(mi, t0)] = s
            return s
        _, wlS, _ = emit_proj(kb, [fc1t, fc2t], xn_dram, cbs13, D, 1024, 0, T,
                              rhs_deps=ln_tok, wbufs=wb13, psums=ps13, ps_state=ps_state,
                              shared=shr, name="s13")

        def cbs3(mi, t0, ps_list, deps):
            o1 = kb.copy("vector", ysb[:], ps_list[0][:], deps=deps + (prev_acc[0],))
            s = kb.dma("gpsimd", acc_dram[128 * mi:128 * (mi + 1), t0:t0 + 512], ysb[:],
                       deps=(o1,), accum_op=ALU.add)
            prev_acc[0] = s
            return s
        sf, _, _ = emit_proj(kb, [fc3t], hh_dram, cbs3, 1024, D, 0, T,
                             rhs_deps=(prev_hh_store[(7, T - 512)],),
                             wbufs=wb2, psums=ps2, ps_state=ps_state, shared=shr, name="s3")

        fin = emit_residual(kb, x, acc_dram, xo, (sf[-1],))
        kb.finalize(fin)
    return nc


def emit_residual(kb, x_dram, acc_dram, xo_dram, acc_tok):
    xa = kb.sb([128, 2048], F32)
    xc = kb.sb([128, 2048], F32)
    fin = []
    prev = None
    for mi in range(8):
        for t0 in range(0, T, 2048):
            l1 = kb.dma("sync", xa[:], x_dram[128 * mi:128 * (mi + 1), t0:t0 + 2048],
                        deps=(prev,))
            l2 = kb.dma("gpsimd", xc[:], acc_dram[128 * mi:128 * (mi + 1), t0:t0 + 2048],
                        deps=tuple(acc_tok) + ((prev,) if prev else ()))
            o = kb.tt("vector", xa[:], xa[:], xc[:], ALU.add, deps=(l1, l2))
            s = kb.dma("scalar", xo_dram[128 * mi:128 * (mi + 1), t0:t0 + 2048], xa[:],
                       deps=(o,))
            prev = s
            fin.append(s)
    return fin


def emit_l0_router(kb, cn, nc, xn_dram, gate_wt, gate_b, ge6_dram, ln_tok, bank_a, bank_b):
    gw = kb.sb([128, 8 * 6], BF16)
    gwl = [kb.dma("gpsimd", gw[:, k * 6:(k + 1) * 6], gate_wt[128 * k:128 * (k + 1), :])
           for k in range(8)]
    gbb = kb.sb([128, 6], F32)
    gbb_tok = kb.dma("sync", gbb[:], gate_b.to_broadcast([128, 6]))
    lg_ps = bank_a
    bank_b_bf = bank_b[:].bitcast(BF16)
    lgT_ps = bank_b_bf[:, 0:24]
    lgf = kb.sb([128, 512], BF16)
    lgtm = kb.sb([128, 24], F32)
    gef = kb.sb([128, 24], BF16)
    geT_ps = bank_b_bf[:, 512:1024]
    gefm = kb.sb([128, 512], F32)
    m1 = kb.sb([128, 1], F32)
    m2 = kb.sb([128, 1], F32)
    den = kb.sb([128, 1], F32)
    msk = kb.sb([128, 6], F32)
    xbq = [kb.sb([128, 512], BF16) for _ in range(8)]
    prev_mm = None
    prev_cycle = None
    last_store = None
    for t0 in range(0, T, 512):
        lds = []
        for k in range(8):
            lds.append(kb.dma("sync", xbq[k][:], xn_dram[128 * k:(k + 1) * 128, t0:t0 + 512],
                              deps=(prev_mm,) + tuple(ln_tok)))
        last = None
        for k in range(8):
            last = kb.mm(gw[:, k * 6:(k + 1) * 6], xbq[k][:], lg_ps[0:6, :],
                         start=(k == 0), stop=(k == 7),
                         deps=(gwl[k], lds[k]) + ((prev_cycle,) if k == 0 else ()))
        prev_mm = last
        cp = kb.copy("vector", lgf[0:6, :], lg_ps[0:6, :], deps=(last,))
        tps = []
        for j in range(4):
            tps.append(kb.transp(lgT_ps[:, j * 6:(j + 1) * 6] if True else None,
                                 lgf[0:6, j * 128:(j + 1) * 128],
                                 cn.ident_bf[0:6, 0:6], deps=(cp, cn.tok[0]) +
                                 ((prev_cycle,) if j == 0 else ())))
        cp2 = kb.copy("vector", lgtm[:], lgT_ps, deps=tuple(tps))
        ge_ready = []
        for j in range(4):
            lgj = lgtm[:, j * 6:(j + 1) * 6]
            a1 = kb.tt("vector", lgj, lgj, gbb[:, 0:6], ALU.add, deps=(cp2, gbb_tok))
            p1 = kb.act(lgj, lgj, AF.Sigmoid, deps=(a1,))
            r1 = kb.red(m1[:, 0:1], lgj, ALU.max, deps=(p1,))
            e1 = kb.tsc("vector", msk[:, 0:6], lgj, m1[:, 0:1], ALU.is_equal, deps=(r1,))
            g1 = kb.tsc("vector", gef[:, j * 6:(j + 1) * 6], msk[:, 0:6], m1[:, 0:1],
                        ALU.mult, deps=(e1,))
            e2 = kb.stt("vector", lgj, msk[:, 0:6], -2.0, lgj, ALU.mult, ALU.add, deps=(g1,))
            r2 = kb.red(m2[:, 0:1], lgj, ALU.max, deps=(e2,))
            m2m = kb.tsc("vector", msk[:, 0:6], lgj, m2[:, 0:1], ALU.is_equal, deps=(r2,))
            g2 = kb.stt("vector", gef[:, j * 6:(j + 1) * 6], msk[:, 0:6], m2[:, 0:1],
                        gef[:, j * 6:(j + 1) * 6], ALU.mult, ALU.add, deps=(m2m,))
            d1 = kb.tt("vector", den[:, 0:1], m1[:, 0:1], m2[:, 0:1], ALU.add, deps=(r2,))
            d2 = kb.tsc("vector", den[:, 0:1], den[:, 0:1], 1e-8, ALU.add, deps=(d1,))
            d3 = kb.op("vector", lambda q, a=den: q.reciprocal(a[:, 0:1], a[:, 0:1]),
                       deps=(d2,))
            g3 = kb.tsc("vector", gef[:, j * 6:(j + 1) * 6], gef[:, j * 6:(j + 1) * 6],
                        den[:, 0:1], ALU.mult, deps=(d3, g2))
            ge_ready.append(g3)
        tps2 = []
        for j in range(4):
            tps2.append(kb.transp(geT_ps[0:6, j * 128:(j + 1) * 128] if True else None,
                                  gef[:, j * 6:(j + 1) * 6], cn.ident_bf[:],
                                  deps=(ge_ready[j], cn.tok[0]) + ((last_store,) if j == 0 else ())))
        cp3 = kb.copy("vector", gefm[0:6, :], geT_ps[0:6, :], deps=tuple(tps2))
        last_store = kb.dma("scalar", ge6_dram[0:6, t0:t0 + 512], gefm[0:6, :], deps=(cp3,))
        prev_cycle = cp3
    return last_store


def build_s1():
    nc = bass.Bass("TRN2", target_bir_lowering=False)
    x = nc.dram_tensor("x", [D, T], F32, kind="ExternalInput").ap()
    Wkt = nc.dram_tensor("Wkt", [D, D], F32, kind="ExternalInput").ap()
    Wvt = nc.dram_tensor("Wvt", [D, D], F32, kind="ExternalInput").ap()
    Wrt = nc.dram_tensor("Wrt", [D, D], F32, kind="ExternalInput").ap()
    Wot = nc.dram_tensor("Wot", [D, D], F32, kind="ExternalInput").ap()
    wv8 = nc.dram_tensor("wv8", [128, 8], F32, kind="ExternalInput").ap()
    uv8 = nc.dram_tensor("uv8", [128, 8], F32, kind="ExternalInput").ap()
    mk8 = nc.dram_tensor("mk8", [128, 8], F32, kind="ExternalInput").ap()
    mv8 = nc.dram_tensor("mv8", [128, 8], F32, kind="ExternalInput").ap()
    mr8 = nc.dram_tensor("mr8", [128, 8], F32, kind="ExternalInput").ap()
    xo = nc.dram_tensor("xo", [D, T], F32, kind="ExternalOutput").ap()

    xn_dram = nc.dram_tensor("xn_scr", [D, T], BF16, kind="Internal").ap()
    xk_dram = nc.dram_tensor("xk_scr", [D, T], BF16, kind="Internal").ap()
    xv_dram = nc.dram_tensor("xv_scr", [D, T], BF16, kind="Internal").ap()
    xr_dram = nc.dram_tensor("xr_scr", [D, T], BF16, kind="Internal").ap()
    k_dram = nc.dram_tensor("k_scr", [D, T], F32, kind="Internal").ap()
    v_dram = nc.dram_tensor("v_scr", [D, T], F32, kind="Internal").ap()
    r_dram = nc.dram_tensor("r_scr", [D, T], BF16, kind="Internal").ap()
    rw_dram = nc.dram_tensor("rw_scr", [D, T], BF16, kind="Internal").ap()

    with contextlib.ExitStack() as ctx:
        kb = KB(nc, ctx)
        cn = Consts(kb)
        pool = [kb.ps([128, 512], F32, f"pool{i}") for i in range(8)]

        ln_out = emit_ln(kb, cn, x, xn_dram, 0, T, (pool[6], pool[7]))
        ln_tok = (ln_out[-1],)

        vecs = kb.sb([128, 8 * 5], F32, "vecs")
        vl = []
        for i, src in enumerate([wv8, uv8, mk8, mv8, mr8]):
            vl.append(kb.dma("sync", vecs[:, i * 8:(i + 1) * 8], src))
        vtok = tuple(vl)

        def wexp_ap(c):
            return wexp[:, c:c + 1]
        wexp = kb.sb([128, 8], F32, "wexp")
        we = kb.act(wexp[:], vecs[:, 0:8], AF.Exp, deps=vtok)
        we2 = kb.tsc("vector", wexp[:], wexp[:], -1.0, ALU.mult, deps=(we,))
        wexp_tok = (we2,)

        TC = 1024
        a_t = kb.sb([128, TC], BF16)
        b_t = kb.sb([128, TC], BF16)
        d_t = kb.sb([128, TC], BF16)
        o_t = kb.sb([128, TC], BF16)
        mix_done = []
        prev = None
        for ci in range(8):
            for t0 in range(0, T, TC):
                l1 = kb.dma("sync", a_t[:], xn_dram[128 * ci:128 * (ci + 1), t0:t0 + TC],
                            deps=(prev,) + ln_tok)
                if t0 == 0:
                    z1 = kb.memset("vector", b_t[:, 0:1], 0.0, deps=(prev,))
                    l2 = kb.dma("sync", b_t[:, 1:TC],
                                xn_dram[128 * ci:128 * (ci + 1), 0:TC - 1], deps=(prev,) + ln_tok)
                    l2d = (l2, z1)
                else:
                    l2 = kb.dma("sync", b_t[:],
                                xn_dram[128 * ci:128 * (ci + 1), t0 - 1:t0 + TC - 1],
                                deps=(prev,) + ln_tok)
                    l2d = (l2,)
                dd = kb.tt("vector", d_t[:], a_t[:], b_t[:], ALU.subtract, deps=(l1,) + l2d)
                s3 = []
                for mi, dst in ((2, xk_dram), (3, xv_dram), (4, xr_dram)):
                    mx = kb.stt("vector", o_t[:], d_t[:], vecs[:, mi * 8 + ci:mi * 8 + ci + 1],
                                b_t[:], ALU.mult, ALU.add, deps=(dd,) + vtok + (prev,))
                    st = kb.dma("scalar", dst[128 * ci:128 * (ci + 1), t0:t0 + TC], o_t[:],
                                deps=(mx,))
                    s3.append(st)
                    prev = st
                mix_done.append(s3[-1])
        mix_tok = (mix_done[-1],)

        kv_sb = kb.sb([128, 512], F32)
        war = {"kv": None}

        def mk_cb(dst, sig=False):
            def cb(mi, t0, ps_list, deps):
                if sig:
                    o1 = kb.act(o_t[:, 0:512], ps_list[0][:], AF.Sigmoid,
                                deps=deps + (war["kv"],))
                    s = kb.dma("scalar", dst[128 * mi:128 * (mi + 1), t0:t0 + 512],
                               o_t[:, 0:512], deps=(o1,))
                else:
                    o1 = kb.copy("vector", kv_sb[:], ps_list[0][:], deps=deps + (war["kv"],))
                    s = kb.dma("scalar", dst[128 * mi:128 * (mi + 1), t0:t0 + 512], kv_sb[:],
                               deps=(o1,))
                war["kv"] = s
                return s
            return cb

        shr = {}
        ps_state = {}
        wb = [None]
        psA = [[pool[0], pool[1]]]
        _, wl, _ = emit_proj(kb, [Wkt], xk_dram, mk_cb(k_dram), D, D, 0, T,
                             rhs_deps=mix_tok, wbufs=wb, psums=psA, ps_state=ps_state,
                             shared=shr, name="k")
        wb[0] = wl[0]
        _, wl, _ = emit_proj(kb, [Wvt], xv_dram, mk_cb(v_dram), D, D, 0, T,
                             rhs_deps=mix_tok, wbufs=wb, psums=psA, ps_state=ps_state,
                             shared=shr, name="v")
        wb[0] = wl[0]
        _, wl, _ = emit_proj(kb, [Wrt], xr_dram, mk_cb(r_dram, sig=True), D, D, 0, T,
                             rhs_deps=mix_tok, wbufs=wb, psums=psA, ps_state=ps_state,
                             shared=shr, name="r")
        wb[0] = wl[0]
        kv_tok = (war["kv"],)

        SC = 1024
        kt = kb.sb([128, SC], F32)
        vt = kb.sb([128, SC], F32)
        o_ext = kb.sb([128, SC + 1], F32)
        p_ext = kb.sb([128, SC + 1], F32)
        q_ext = kb.sb([128, SC + 1], F32)
        A2 = kb.sb([128, SC], F32)
        B2 = kb.sb([128, SC], F32)
        BV = kb.sb([128, SC], F32)
        uk = kb.sb([128, SC], F32)
        no = kb.sb([128, SC], F32)
        Aa = kb.sb([128, SC], F32)
        Bc = kb.sb([128, SC], F32)
        num = kb.sb([128, SC], F32)
        den = kb.sb([128, SC], F32)
        rt = kb.sb([128, SC], BF16)
        rw = kb.sb([128, SC], BF16)
        carry = kb.sb([128, 3], F32)
        scan_done = []
        prev_sc = None
        for ci in range(8):
            ini = kb.memset("vector", carry[:, 0:1], -1e30, deps=(prev_sc,))
            ini2 = kb.memset("vector", carry[:, 1:3], 0.0, deps=(prev_sc,))
            cprev = (ini, ini2)
            for t0 in range(0, T, SC):
                lk = kb.dma("sync", kt[:], k_dram[128 * ci:128 * (ci + 1), t0:t0 + SC],
                            deps=kv_tok + (prev_sc,))
                lv = kb.dma("sync", vt[:], v_dram[128 * ci:128 * (ci + 1), t0:t0 + SC],
                            deps=kv_tok + (prev_sc,))
                lr = kb.dma("sync", rt[:], r_dram[128 * ci:128 * (ci + 1), t0:t0 + SC],
                            deps=kv_tok + (prev_sc,))
                c0 = kb.copy("vector", o_ext[:, 0:1], carry[:, 0:1], deps=cprev + (prev_sc,))
                s1 = kb.scan(o_ext[:, 1:SC + 1],
                             wexp[:, ci:ci + 1].to_broadcast([128, SC]), kt[:],
                             carry[:, 0:1], ALU.add, ALU.max,
                             deps=(lk, c0) + wexp_tok)
                a1 = kb.stt("vector", A2[:], o_ext[:, 0:SC], wexp[:, ci:ci + 1],
                            o_ext[:, 1:SC + 1], ALU.add, ALU.subtract, deps=(s1,))
                a2 = kb.act(A2[:], A2[:], AF.Exp, deps=(a1,))
                b1 = kb.tt("vector", B2[:], kt[:], o_ext[:, 1:SC + 1], ALU.subtract,
                           deps=(s1, lk))
                b2 = kb.act(B2[:], B2[:], AF.Exp, deps=(b1,))
                bv = kb.tt("vector", BV[:], B2[:], vt[:], ALU.mult, deps=(b2, lv))
                c1 = kb.copy("vector", p_ext[:, 0:1], carry[:, 1:2], deps=cprev + (prev_sc,))
                s2 = kb.scan(p_ext[:, 1:SC + 1], A2[:], BV[:], carry[:, 1:2],
                             ALU.mult, ALU.add, deps=(a2, bv, c1))
                c2 = kb.copy("vector", q_ext[:, 0:1], carry[:, 2:3], deps=cprev + (prev_sc,))
                s3 = kb.scan(q_ext[:, 1:SC + 1], A2[:], B2[:], carry[:, 2:3],
                             ALU.mult, ALU.add, deps=(a2, b2, c2))
                n1 = kb.copy("vector", carry[:, 0:1], o_ext[:, SC:SC + 1], deps=(s1,))
                n2 = kb.copy("vector", carry[:, 1:2], p_ext[:, SC:SC + 1], deps=(s2,))
                n3 = kb.copy("vector", carry[:, 2:3], q_ext[:, SC:SC + 1], deps=(s3,))
                cprev = (n1, n2, n3)
                u1 = kb.tsc("vector", uk[:], kt[:], vecs[:, 8 + ci:8 + ci + 1], ALU.add,
                            deps=(lk,) + vtok)
                x1 = kb.tt("vector", no[:], o_ext[:, 0:SC], uk[:], ALU.max, deps=(u1, s1))
                x2 = kb.tt("vector", Aa[:], o_ext[:, 0:SC], no[:], ALU.subtract, deps=(x1,))
                x3 = kb.act(Aa[:], Aa[:], AF.Exp, deps=(x2,))
                x4 = kb.tt("vector", Bc[:], uk[:], no[:], ALU.subtract, deps=(x1, u1))
                x5 = kb.act(Bc[:], Bc[:], AF.Exp, deps=(x4,))
                y1 = kb.tt("vector", num[:], Aa[:], p_ext[:, 0:SC], ALU.mult, deps=(x3, s2))
                y2 = kb.tt("vector", den[:], Bc[:], vt[:], ALU.mult, deps=(x5, lv))
                y3 = kb.tt("vector", num[:], num[:], den[:], ALU.add, deps=(y1, y2))
                y4 = kb.tt("vector", den[:], Aa[:], q_ext[:, 0:SC], ALU.mult, deps=(x3, s3, y3))
                y5 = kb.stt("vector", den[:], den[:], 1.0, Bc[:], ALU.mult, ALU.add,
                            deps=(y4, x5))
                y6 = kb.act(den[:], den[:], AF.Ln, deps=(y5,))
                y7 = kb.act(den[:], den[:], AF.Exp, scale=-1.0, deps=(y6,))
                z1 = kb.tt("vector", num[:], num[:], den[:], ALU.mult, deps=(y3, y7))
                z2 = kb.tt("vector", rw[:], num[:], rt[:], ALU.mult, deps=(z1, lr))
                st = kb.dma("scalar", rw_dram[128 * ci:128 * (ci + 1), t0:t0 + SC], rw[:],
                            deps=(z2,))
                prev_sc = st
                scan_done.append(st)
        scan_tok = (scan_done[-1],)

        xres = kb.sb([128, 512], F32)
        war2 = {"o": None}

        def cb_out(mi, t0, ps_list, deps):
            lx = kb.dma("sync", xres[:], x[128 * mi:128 * (mi + 1), t0:t0 + 512],
                        deps=deps + (war2["o"],))
            o1 = kb.tt("vector", xres[:], xres[:], ps_list[0][:], ALU.add, deps=(lx,) + deps)
            s = kb.dma("scalar", xo[128 * mi:128 * (mi + 1), t0:t0 + 512], xres[:],
                       deps=(o1,))
            war2["o"] = s
            return s
        fin, _, _ = emit_proj(kb, [Wot], rw_dram, cb_out, D, D, 0, T,
                              rhs_deps=scan_tok, wbufs=wb, psums=psA, ps_state=ps_state,
                              shared=shr, name="o")
        kb.finalize(fin)
    return nc


def build_s7():
    H, DH, C = 16, 64, 128
    NCH = T // C
    nc = bass.Bass("TRN2", target_bir_lowering=False)
    x = nc.dram_tensor("x", [D, T], F32, kind="ExternalInput").ap()
    Wwt = nc.dram_tensor("Wwt", [D, D], F32, kind="ExternalInput").ap()
    Wot = nc.dram_tensor("Wot", [D, D], F32, kind="ExternalInput").ap()
    Wgba = nc.dram_tensor("Wgba", [D, 48], F32, kind="ExternalInput").ap()
    Alog = nc.dram_tensor("Alog", [16, 1], F32, kind="ExternalInput").ap()
    dtb = nc.dram_tensor("dtb", [16, 1], F32, kind="ExternalInput").ap()
    xo = nc.dram_tensor("xo", [D, T], F32, kind="ExternalOutput").ap()

    xn_dram = nc.dram_tensor("xn_scr", [D, T], BF16, kind="Internal").ap()
    xntok = nc.dram_tensor("xntok_scr", [T, D], F32, kind="Internal").ap()
    vtok_dram = nc.dram_tensor("vtok_scr", [T, D], F32, kind="Internal").ap()
    gb_tok = nc.dram_tensor("gbtok_scr", [T, 32], F32, kind="Internal").ap()
    dec_dram = nc.dram_tensor("dec_scr", [16, T], F32, kind="Internal").ap()
    cdec_dram = nc.dram_tensor("cdec_scr", [2, T], F32, kind="Internal").ap()
    edl_dram = nc.dram_tensor("edl_scr", [NCH, 16], F32, kind="Internal").ap()
    o_dram = nc.dram_tensor("o_scr", [D, T], BF16, kind="Internal").ap()

    with contextlib.ExitStack() as ctx:
        kb = KB(nc, ctx)
        cn = Consts(kb)
        pool = [kb.ps([128, 512], F32, f"pool{i}") for i in range(8)]

        ln_out = emit_ln(kb, cn, x, xn_dram, 0, T, (pool[6], pool[7]))
        ln_tok = (ln_out[-1],)

        idf = kb.sb([128, 128], F32, "idf")
        idl = kb.copy("vector", idf[:], cn.ident_bf[:], deps=(cn.tok[0],))

        xtile = kb.sb([128, 128], F32)
        ttile = kb.sb([128, 128], F32)
        prev_t = None
        xk_tok = None
        for ti in range(T // 128):
            for mi in range(8):
                l1 = kb.dma("gpsimd", xtile[:], xn_dram[mi * 128:(mi + 1) * 128, ti * 128:(ti + 1) * 128],
                            deps=ln_tok + ((prev_t,) if prev_t else ()))
                tp = kb.transp(pool[0][0:128, 0:128], xtile[:], idf[:], deps=(l1, idl, prev_t))
                cp = kb.copy("vector", ttile[:], pool[0][0:128, 0:128], deps=(tp,))
                st = kb.dma("scalar", xntok[ti * 128:(ti + 1) * 128, mi * 128:(mi + 1) * 128],
                            ttile[:], deps=(cp,))
                prev_t = st
        xntok_tok = (prev_t,)

        gvec = kb.sb([16, 2], F32)
        gl1 = kb.dma("sync", gvec[0:16, 0:1], Alog)
        gl2 = kb.dma("sync", gvec[0:16, 1:2], dtb)
        negA = kb.sb([16, 1], F32)
        na1 = kb.act(negA[0:16, :], gvec[0:16, 0:1], AF.Exp, deps=(gl1,))
        na2 = kb.tsc("vector", negA[0:16, :], negA[0:16, :], -1.0, ALU.mult, deps=(na1,))

        gw = kb.sb([128, 8 * 48], BF16)
        gwl = [kb.dma("gpsimd", gw[:, k * 48:(k + 1) * 48], Wgba[128 * k:128 * (k + 1), :])
               for k in range(8)]
        xbq = [kb.sb([128, 512], BF16) for _ in range(8)]
        gps = pool[2]
        gsb = kb.sb([128, 512], F32)
        sp1 = kb.sb([16, 512], F32)
        sp2 = kb.sb([16, 512], F32)
        gbT_sb = kb.sb([128, 32], F32)
        prev_rd = ()
        prev_st = None
        prev_dec = None
        gd_tok = []
        for t0 in range(0, T, 512):
            lds = [kb.dma("sync", xbq[k][:], xn_dram[128 * k:(k + 1) * 128, t0:t0 + 512],
                          deps=prev_rd + ln_tok) for k in range(8)]
            last = None
            for k in range(8):
                last = kb.mm(gw[:, k * 48:(k + 1) * 48], xbq[k][:], gps[0:48, :],
                             start=(k == 0), stop=(k == 7),
                             deps=(gwl[k], lds[k]) + (prev_rd if k == 0 else ()))
            sg = kb.act(gsb[0:32, :], gps[0:32, :], AF.Sigmoid, deps=(last, prev_st))
            z1 = kb.tsc("vector", sp1[0:16, :], gps[32:48, :], gvec[0:16, 1:2], ALU.add,
                        deps=(last, gl2, prev_st, prev_dec))
            prev_rd = (sg, z1)
            z2 = kb.op("scalar", lambda q, a=sp2, b=sp1: q.activation(a[0:16, :], b[0:16, :],
                       AF.Abs), deps=(z1,))
            z3 = kb.act(sp2[0:16, :], sp2[0:16, :], AF.Exp, scale=-1.0, deps=(z2,))
            z4 = kb.act(sp2[0:16, :], sp2[0:16, :], AF.Ln, bias=1.0, deps=(z3,))
            z5 = kb.tsc("vector", sp1[0:16, :], sp1[0:16, :], 0.0, ALU.max, deps=(z1, z4))
            z6 = kb.tt("vector", sp1[0:16, :], sp1[0:16, :], sp2[0:16, :], ALU.add, deps=(z5,))
            z7 = kb.tsc("vector", sp1[0:16, :], sp1[0:16, :], negA[0:16, 0:1], ALU.mult,
                        deps=(z6, na2))
            s1 = kb.dma("scalar", dec_dram[0:16, t0:t0 + 512], sp1[0:16, :], deps=(z7,))
            prev_dec = s1
            sts = []
            prev_cp = None
            prev_stj = prev_st
            for j in range(4):
                tp = kb.transp(pool[1][0:128, 0:32],
                               gsb[0:32, j * 128:(j + 1) * 128], idf[0:32, 0:32],
                               deps=(sg, idl, prev_cp) + ((prev_st,) if j == 0 else ()))
                cp = kb.copy("vector", gbT_sb[:], pool[1][0:128, 0:32], deps=(tp, prev_stj))
                st = kb.dma("scalar", gb_tok[t0 + j * 128:t0 + (j + 1) * 128, 0:32], gbT_sb[:],
                            deps=(cp,))
                prev_cp = cp
                prev_stj = st
                sts.append(st)
            prev_st = sts[-1]
            gd_tok.append(s1)
            gd_tok.append(sts[-1])
        gates_tok = (gd_tok[-1], gd_tok[-2])

        vw = kb.sb([128, 8 * D], BF16, "vw")
        vwl = [kb.dma("gpsimd", vw[:, k * D:(k + 1) * D], Wwt[128 * k:(k + 1) * 128, :])
               for k in range(8)]
        btile = kb.sb([128, 16], F32)
        vsb = kb.sb([128, 512], F32)
        xs8 = kb.sb([128, 8 * 128], BF16, "xs8")
        prev_v = None
        vfin = []
        for ti in range(T // 128):
            lxs = [kb.dma("sync", xs8[:, k * 128:(k + 1) * 128],
                          xn_dram[k * 128:(k + 1) * 128, ti * 128:(ti + 1) * 128],
                          deps=(prev_v,) + ln_tok) for k in range(8)]
            lbt = kb.dma("sync", btile[:], gb_tok[ti * 128:(ti + 1) * 128, 16:32],
                         deps=gates_tok + ((prev_v,) if prev_v else ()))
            for nj in range(2):
                last = None
                for k in range(8):
                    last = kb.mm(xs8[:, k * 128:(k + 1) * 128], vw[:, k * D + nj * 512:k * D + nj * 512 + 512],
                                 pool[3][:], start=(k == 0), stop=(k == 7),
                                 deps=(lxs[k], vwl[k]) + ((prev_v,) if k == 0 else ()))
                bm = kb.tt("vector", vsb[:],
                           pool[3][:].rearrange("p (h e) -> p h e", e=64),
                           btile[:, nj * 8:nj * 8 + 8].unsqueeze(2).to_broadcast([128, 8, 64]),
                           ALU.mult, deps=(last, lbt))
                sv = kb.dma("scalar", vtok_dram[ti * 128:(ti + 1) * 128, nj * 512:nj * 512 + 512],
                            vsb[:], deps=(bm,))
                prev_v = sv
                vfin.append(sv)
        v_tok = (vfin[-1],)

        rd_init = {"p3": vfin[-1], "p4a": None, "p4b": None, "p5a": None,
                   "p5b": None, "p6": ln_tok[0], "p7": ln_tok[0]}
        ofin = _s7_scan_phase(kb, cn, nc, pool, idf, idl, xn_dram, xntok, vtok_dram,
                              gb_tok, dec_dram, o_dram, v_tok, gates_tok, xntok_tok,
                              rd_init)
        o_tok = (ofin[-1],)

        xres = kb.sb([128, 512], F32)
        xnres = kb.sb([128, 512], F32)
        war2 = {"o": None}

        def cb_out(mi, t0, ps_list, deps):
            lx = kb.dma("sync", xres[:], x[128 * mi:128 * (mi + 1), t0:t0 + 512],
                        deps=deps + (war2["o"],))
            lxn = kb.dma("gpsimd", xnres[:], xn_dram[128 * mi:128 * (mi + 1), t0:t0 + 512],
                         deps=deps + (war2["o"],) + ln_tok)
            a1 = kb.tt("vector", xres[:], xres[:], xnres[:], ALU.add, deps=(lx, lxn))
            a2 = kb.tt("vector", xres[:], xres[:], ps_list[0][:], ALU.add, deps=(a1,) + deps)
            st = kb.dma("scalar", xo[128 * mi:128 * (mi + 1), t0:t0 + 512], xres[:],
                        deps=(a2,))
            war2["o"] = st
            return st
        psO = [[pool[0], pool[1]]]
        fin, _, _ = emit_proj(kb, [Wot], o_dram, cb_out, D, D, 0, T,
                              rhs_deps=o_tok, psums=psO,
                              ps_state={id(pool[0]): xntok_tok[0], id(pool[1]): prev_st},
                              name="wo")
        kb.finalize(fin)
    return nc

def _s7_scan_phase(kb, cn, nc, pool, idf, idl, xn_dram, xntok, vtok_dram, gb_tok,
                   dec_dram, o_dram, v_tok, gates_tok, xntok_tok, rd_init):
    H, DH, C = 16, 64, 128
    NCH = T // C
    S_sb = kb.sb([64, 16 * 64], F32, "S_state")
    rkF = kb.sb([64, 16 * C], F32, "rkF")
    wkF = kb.sb([64, 16 * C], F32, "wkF")
    vch = kb.sb([128, 1024], F32, "vch")
    wktokc = kb.sb([128, 1024], F32, "wktokc")
    dch = kb.sb([16, C], F32)
    cdec = kb.sb([16, C], F32)
    dstage = kb.sb([2, 16 * C], F32, "dstage")
    rstage = kb.sb([2, 16 * C], F32, "rstage")
    dexp16 = kb.sb([16, C], F32, "dexp16")
    dw16 = kb.sb([16, C], F32, "dw16")
    trosb = kb.sb([128, 32], F32)
    edl = kb.sb([64, 16], F32)
    gtile = kb.sb([128, 16], F32)
    expd = kb.sb([128, C], F32)
    intraT = kb.sb([128, C], F32)
    o_sb = kb.sb([128, 64], F32)
    vtld = kb.sb([128, 64], F32)
    otr = kb.sb([64, 128], BF16)
    mask_su = kb.sb([128, C], F32, "mask_su")
    ninf = kb.sb([16, 1], F32)
    cdec_sc = nc.dram_tensor("cdecsc", [1, 16 * C], F32, kind="Internal").ap()
    edl_sc = nc.dram_tensor("edlsc", [1, 16], F32, kind="Internal").ap()

    ms = kb.iota(mask_su[:], [[1, C]], base=0, chmul=-1)
    m2a = kb.tsc("vector", mask_su[:], mask_su[:], 0.5, ALU.is_ge, deps=(ms,))
    m2b = kb.tsc("vector", mask_su[:], mask_su[:], -1.0, ALU.add, deps=(m2a,))
    m2 = kb.tsc("vector", mask_su[:], mask_su[:], 3e38, ALU.mult, deps=(m2b,))
    z0 = kb.memset("vector", S_sb[:], 0.0)
    on1 = kb.memset("vector", dstage[0:2, :], 1.0)
    on2 = kb.memset("vector", rstage[0:2, :], -1.0)
    nf = kb.memset("vector", ninf[:], -3e38)
    mask_tok = (m2, z0, on1, on2, nf)

    fin = []
    prev_use = mask_tok
    prev_edl = None
    prev_tro = None
    rd = dict(rd_init)
    for n in range(NCH):
        t0 = n * C
        lrk = [kb.dma("gpsimd", rkF[:, k * C:(k + 1) * C],
                      xn_dram[k * 64:(k + 1) * 64, t0:t0 + C], deps=prev_use)
               for k in range(16)]
        if n == 0:
            zk8 = [kb.memset("vector", wkF[:, k * C:k * C + 1], 0.0, deps=prev_use)
                   for k in range(16)]
            lwk = [kb.dma("gpsimd", wkF[:, k * C + 1:(k + 1) * C],
                          xn_dram[k * 64:(k + 1) * 64, 0:C - 1], deps=prev_use)
                   for k in range(16)]
            zk2 = kb.memset("vector", wktokc[0:1, :], 0.0, deps=prev_use)
            lwt = kb.dma("gpsimd", wktokc[1:128, :], xntok[0:C - 1, :], deps=prev_use + xntok_tok)
            lwk = lwk + zk8 + [zk2]
        else:
            lwk = [kb.dma("gpsimd", wkF[:, k * C:(k + 1) * C],
                          xn_dram[k * 64:(k + 1) * 64, t0 - 1:t0 + C - 1], deps=prev_use)
                   for k in range(16)]
            lwt = kb.dma("gpsimd", wktokc[:], xntok[t0 - 1:t0 + C - 1, :],
                         deps=prev_use + xntok_tok)
        lv = kb.dma("sync", vch[:], vtok_dram[t0:t0 + C, :], deps=prev_use + v_tok)
        lg = kb.dma("sync", gtile[:], gb_tok[t0:t0 + C, 0:16], deps=prev_use + gates_tok)
        ldc = kb.dma("sync", dch[0:16, :], dec_dram[0:16, t0:t0 + C],
                     deps=prev_use + gates_tok)
        c1 = kb.scan(cdec[0:16, :], dch[0:16, :], ninf[0:16, 0:1].to_broadcast([16, C]),
                     0.0, ALU.add, ALU.max, deps=(ldc,) + prev_use + (nf,))
        sc1 = kb.dma("scalar", cdec_sc.rearrange("a (h t) -> (a h) t", h=16), cdec[0:16, :],
                     deps=(c1,) + prev_use)
        ld1 = kb.dma("sync", dstage[0:1, :], cdec_sc, deps=(sc1,) + prev_use + mask_tok)
        ld2 = kb.dma("sync", rstage[1:2, :], cdec_sc, deps=(sc1,) + prev_use + mask_tok)
        e1 = kb.act(dexp16[0:16, :], cdec[0:16, :], AF.Exp,
                    deps=(c1,) + ((prev_tro,) if prev_tro else ()))
        w1 = kb.tsc("vector", dw16[0:16, :], cdec[0:16, :], cdec[0:16, C - 1:C],
                    ALU.subtract, deps=(c1,) + ((prev_tro,) if prev_tro else ()))
        w2 = kb.tsc("vector", dw16[0:16, :], dw16[0:16, :], -1.0, ALU.mult, deps=(w1,))
        w3 = kb.act(dw16[0:16, :], dw16[0:16, :], AF.Exp, deps=(w2,))
        tpd1 = kb.transp(pool[3][0:128, 0:16], dexp16[0:16, :], idf[0:16, 0:16],
                         deps=(e1, idl, rd["p3"]))
        tpd2 = kb.transp(pool[3][0:128, 16:32], dw16[0:16, :], idf[0:16, 0:16],
                         deps=(w3, idl, rd["p3"]))
        tro = kb.copy("vector", trosb[:], pool[3][0:128, 0:32], deps=(tpd1, tpd2))
        rd["p3"] = tro
        prev_tro = tro
        se1 = kb.dma("scalar", edl_sc.rearrange("a h -> h a"), dexp16[0:16, C - 1:C],
                     deps=(e1, prev_edl))
        le1 = kb.dma("sync", edl[:], edl_sc.to_broadcast([64, 16]), deps=(se1,))
        prev_edl = le1
        hdeps = (tro, le1, lv, lg, ld1, ld2) + tuple(lrk) + tuple(lwk) + (lwt,)
        new_use = []
        for h in range(16):
            rk_h = rkF[0:64, h * C:(h + 1) * C]
            wk_h = wkF[0:64, h * C:(h + 1) * C]
            p1 = kb.mm(wk_h, rk_h, pool[4][:, 0:128], start=True, stop=True,
                       deps=hdeps + (rd["p4a"],))
            d1 = kb.mm(dstage[0:2, h * C:(h + 1) * C], rstage[0:2, h * C:(h + 1) * C],
                       pool[5][:, 0:128], start=True, stop=True, deps=hdeps + (rd["p5a"],))
            ma = kb.tt("vector", expd[:, 0:C], pool[5][:, 0:128], mask_su[:], ALU.add,
                       deps=(d1,) + mask_tok)
            rd["p5a"] = ma
            ex = kb.act(expd[:, 0:C], expd[:, 0:C], AF.Exp, deps=(ma,))
            i2 = kb.tt("vector", intraT[:], pool[4][:, 0:128], expd[:], ALU.mult,
                       deps=(p1, ex))
            rd["p4a"] = i2
            i1 = i2
            vh = vch[:, h * 64:(h + 1) * 64]
            o1 = kb.mm(rk_h, S_sb[:, h * 64:(h + 1) * 64], pool[6][0:128, 0:64],
                       start=True, stop=True, deps=hdeps + (z0, rd["p6"]))
            o2 = kb.mm(intraT[:], vh, pool[7][0:128, 0:64], start=True, stop=True,
                       deps=(i2, lv, rd["p7"]))
            c6 = kb.copy("scalar", o_sb[:], pool[6][0:128, 0:64], deps=(o1,))
            e2 = kb.stt("vector", o_sb[:], o_sb[:], trosb[:, h:h + 1],
                        pool[7][0:128, 0:64], ALU.mult, ALU.add, deps=(c6, o2, tro))
            rd["p6"] = c6
            rd["p7"] = e2
            g1 = kb.tsc("vector", o_sb[:], o_sb[:], gtile[:, h:h + 1], ALU.mult,
                        deps=(e2, lg))
            tp2 = kb.transp(pool[4][0:64, 128:256], o_sb[:], idf[:],
                            deps=(g1, i1, rd["p4b"]))
            oc = kb.copy("vector", otr[:], pool[4][0:64, 128:256],
                         deps=(tp2, rd.get("so")))
            rd["p4b"] = oc
            so = kb.dma("scalar",
                        o_dram[h * 64:(h + 1) * 64, t0:t0 + C], otr[:],
                        deps=(oc,))
            rd["so"] = so
            fin.append(so)
            sv1 = kb.tsc("vector", vtld[:], vh, trosb[:, 16 + h:16 + h + 1], ALU.mult,
                         deps=(lv, tro, oc))
            su = kb.mm(wktokc[:, h * 64:(h + 1) * 64], vtld[:], pool[5][0:64, 128:192],
                       start=True, stop=True, deps=(sv1, lwt, ma, rd["p5b"]))
            sm = kb.stt("vector", S_sb[:, h * 64:(h + 1) * 64],
                        S_sb[:, h * 64:(h + 1) * 64], edl[:, h:h + 1],
                        pool[5][0:64, 128:192], ALU.mult, ALU.add,
                        deps=(su, le1, o1))
            rd["p5b"] = sm
            new_use.append(sm)
            new_use.append(so)
        prev_use = tuple(new_use[-4:]) + (i2, sv1)
    return fin


def build_s5():
    HH, DH, SEG, P = 16, 64, 512, 16
    nc = bass.Bass("TRN2", target_bir_lowering=False)
    x = nc.dram_tensor("x", [D, T], F32, kind="ExternalInput").ap()
    Wqkv = nc.dram_tensor("Wqkv", [D, 3072], F32, kind="ExternalInput").ap()
    Wout = nc.dram_tensor("Wout", [1024, D], F32, kind="ExternalInput").ap()
    gn8 = nc.dram_tensor("gn8", [128, 8], F32, kind="ExternalInput").ap()
    pmkT = nc.dram_tensor("pmkT", [64, 16 * 16], F32, kind="ExternalInput").ap()
    pmv = nc.dram_tensor("pmv", [16 * 16, 64], F32, kind="ExternalInput").ap()
    costab = nc.dram_tensor("costab", [128, T], F32, kind="ExternalInput").ap()
    sintab = nc.dram_tensor("sintab", [128, T], F32, kind="ExternalInput").ap()
    rotm = nc.dram_tensor("rotm", [128, 128], F32, kind="ExternalInput").ap()
    xo = nc.dram_tensor("xo", [D, T], F32, kind="ExternalOutput").ap()

    xn0_dram = nc.dram_tensor("xn0_scr", [D, T], BF16, kind="Internal").ap()
    xn_dram = nc.dram_tensor("xn_scr", [D, T], BF16, kind="Internal").ap()
    q_dram = nc.dram_tensor("q_scr", [D, T], BF16, kind="Internal").ap()
    k_dram = nc.dram_tensor("k_scr", [D, T], BF16, kind="Internal").ap()
    vtok_dram = nc.dram_tensor("vtok_scr", [T, D], BF16, kind="Internal").ap()
    o_dram = nc.dram_tensor("o_scr", [D, T], BF16, kind="Internal").ap()
    mrow_dram = nc.dram_tensor("mrow_scr", [2, SEG], F32, kind="Internal").ap()

    with contextlib.ExitStack() as ctx:
        kb = KB(nc, ctx)
        cn = Consts(kb)
        pool = [kb.ps([128, 512], F32, f"pool{i}") for i in range(4)]

        gnv = kb.sb([128, 8], F32)
        gnl = kb.dma("sync", gnv[:], gn8)
        ln0_out = emit_ln(kb, cn, x, xn0_dram, 0, T, (pool[0], pool[1]))
        ln_out = emit_ln(kb, cn, xn0_dram, xn_dram, 0, T, (pool[0], pool[1]),
                         w_ap=lambda i: gnv[:, i:i + 1], wb_deps=(gnl,),
                         eps=1.1920929e-07, rms=True, x_deps=(ln0_out[-1],),
                         psum_deps=(ln0_out[-1],))
        ln_tok = (ln_out[-1],)

        cosb = kb.sb([128, T // 16], F32, "cosb")
        sinb = kb.sb([128, T // 16], F32, "sinb")
        rotf = kb.sb([128, 128], F32, "rotf")
        rl = kb.dma("sync", rotf[:], rotm)

        qf = kb.sb([128, 512], F32)
        rot_s = kb.sb([128, 512], F32)
        qo = kb.sb([128, 512], BF16)
        war = {"q": None, "cs": None}

        def mk_qk_cb(dst, scale):
            def cb(mi, t0, ps_list, deps):
                lc = kb.dma("sync", cosb[:, 0:512], costab[:, t0:t0 + 512],
                            deps=(war["cs"],))
                ls = kb.dma("sync", sinb[:, 0:512], sintab[:, t0:t0 + 512],
                            deps=(war["cs"],))
                c1 = kb.tsc("vector", qf[:], ps_list[0][:], scale, ALU.mult,
                            deps=deps + (war["q"],))
                r1 = kb.mm(rotf[:], qf[:], pool[3][:], start=True, stop=True,
                           deps=(c1, rl, war.get("rq")))
                m1_ = kb.tt("vector", rot_s[:], pool[3][:], sinb[:, 0:512], ALU.mult,
                            deps=(r1, ls))
                war["rq"] = m1_
                m2_ = kb.tt("vector", qf[:], qf[:], cosb[:, 0:512], ALU.mult,
                            deps=(c1, lc, m1_))
                m3_ = kb.tt("vector", qo[:], qf[:], rot_s[:], ALU.add, deps=(m2_,))
                st = kb.dma("scalar", dst[128 * mi:128 * (mi + 1), t0:t0 + 512], qo[:],
                            deps=(m3_,))
                war["q"] = st
                war["cs"] = m2_
                return st
            return cb

        shr = {}
        ps_state = {}
        psA = [[pool[0], pool[1]]]
        ps_state[id(pool[0])] = ln_out[-1]
        ps_state[id(pool[1])] = ln_out[-1]
        wbq = [None]
        _, wlq, _ = emit_proj(kb, [Wqkv[:, 0:1024]], xn_dram, mk_qk_cb(q_dram, 0.125),
                              D, 1024, 0, T, rhs_deps=ln_tok, wbufs=wbq, psums=psA,
                              ps_state=ps_state, shared=shr, name="q")
        wbq[0] = wlq[0]
        _, wlq, _ = emit_proj(kb, [Wqkv[:, 1024:2048]], xn_dram, mk_qk_cb(k_dram, 1.0),
                              D, 1024, 0, T, rhs_deps=ln_tok, wbufs=wbq, psums=psA,
                              ps_state=ps_state, shared=shr, name="k")
        wbq[0] = wlq[0]
        qk_tok = (war["q"],)

        vw = kb.sb([128, 8 * 1024], BF16, "vw")
        vwl = [kb.dma("gpsimd", vw[:, kk * 1024:(kk + 1) * 1024],
                      Wqkv[128 * kk:(kk + 1) * 128, 2048:3072]) for kk in range(8)]
        xs8 = kb.sb([128, 8 * 128], BF16, "xs8v")
        vsb = kb.sb([128, 512], BF16)
        prev_v = None
        vfin = []
        for ti in range(T // 128):
            lxs = [kb.dma("sync", xs8[:, kk * 128:(kk + 1) * 128],
                          xn_dram[kk * 128:(kk + 1) * 128, ti * 128:(ti + 1) * 128],
                          deps=((prev_v,) if prev_v else ()) + ln_tok) for kk in range(8)]
            for nj in range(2):
                last = None
                for kk in range(8):
                    last = kb.mm(xs8[:, kk * 128:(kk + 1) * 128],
                                 vw[:, kk * 1024 + nj * 512:kk * 1024 + nj * 512 + 512],
                                 pool[2][:], start=(kk == 0), stop=(kk == 7),
                                 deps=(lxs[kk], vwl[kk]) + ((prev_v,) if kk == 0 else ()))
                cv = kb.copy("vector", vsb[:], pool[2][:], deps=(last,))
                last_cv = cv
                sv = kb.dma("scalar", vtok_dram[ti * 128:(ti + 1) * 128, nj * 512:nj * 512 + 512],
                            vsb[:], deps=(cv,))
                prev_v = sv
                vfin.append(sv)
        v_tok = (vfin[-1],)

        negC_row = kb.sb([1, 16], F32, "negCrow")
        negC_sb = kb.sb([128, 16], F32, "negCsb")
        negC_dram = nc.dram_tensor("negC_scr", [1, 16], F32, kind="Internal").ap()
        sqt = kb.sb([64, 2048], BF16)
        hmax = kb.sb([1, 8], F32)
        lnq = kb.sb([1, 16], F32)
        lnk = kb.sb([1, 16], F32)
        onescol = kb.sb([64, 1], BF16)
        oc1 = kb.memset("vector", onescol[:], 1.0)
        rdC = {"sq": None, "nr": war.get("rq")}
        lastmv = None
        for h in range(16):
            for which, dram, dstrow in ((0, q_dram, lnq), (1, k_dram, lnk)):
                mv = None
                for ci4 in range(4):
                    t0 = ci4 * 2048
                    lqh = kb.dma("sync", sqt[:], dram[h * 64:(h + 1) * 64, t0:t0 + 2048],
                                 deps=qk_tok + (rdC["sq"],))
                    sq1 = kb.act(sqt[:], sqt[:], AF.Square, deps=(lqh,))
                    subs = []
                    for jj in range(4):
                        nmj = kb.mm(onescol[:], sqt[:, jj * 512:(jj + 1) * 512],
                                    pool[3][0:1, :], start=True, stop=True,
                                    deps=(sq1, oc1, rdC["nr"]))
                        rmj = kb.red(hmax[0:1, jj:jj + 1], pool[3][0:1, :], ALU.max,
                                     deps=(nmj,))
                        rdC["nr"] = rmj
                        subs.append(rmj)
                    rdC["sq"] = nmj
                    rm2 = kb.red(hmax[0:1, 4:5], hmax[0:1, 0:4], ALU.max, deps=tuple(subs))
                    if ci4 == 0:
                        mv = kb.copy("vector", dstrow[0:1, h:h + 1], hmax[0:1, 4:5],
                                     deps=(rm2, lastmv))
                    else:
                        mv = kb.tt("vector", dstrow[0:1, h:h + 1], dstrow[0:1, h:h + 1],
                                   hmax[0:1, 4:5], ALU.max, deps=(rm2, mv))
                lastmv = mv
        lv1 = kb.act(lnq[0:1, :], lnq[0:1, :], AF.Ln, deps=(lastmv,))
        lv2 = kb.act(lnk[0:1, :], lnk[0:1, :], AF.Ln, deps=(lastmv, lv1))
        cs = kb.tt("vector", negC_row[0:1, :], lnq[0:1, :], lnk[0:1, :], ALU.add,
                   deps=(lv1, lv2))
        ex2 = kb.act(negC_row[0:1, :], negC_row[0:1, :], AF.Exp, scale=0.5, deps=(cs,))
        ng2 = kb.tsc("vector", negC_row[0:1, :], negC_row[0:1, :], -1.0, ALU.mult,
                     deps=(ex2,))
        sC = kb.dma("scalar", negC_dram, negC_row[0:1, :], deps=(ng2,))
        lC = kb.dma("sync", negC_sb[:], negC_dram.to_broadcast([128, 16]), deps=(sC,))

        mpassall = kb.sb([128, 2048], F32, "mpassall")
        mtoks = []
        for c_i in range(4):
            aa = kb.iota(mpassall[:, c_i * 512:(c_i + 1) * 512], [[1, 512]],
                         base=-c_i * 128, chmul=-1)
            bb = kb.tsc("vector", mpassall[:, c_i * 512:(c_i + 1) * 512],
                        mpassall[:, c_i * 512:(c_i + 1) * 512], -0.5, ALU.is_ge,
                        deps=(aa,))
            cc = kb.tsc("vector", mpassall[:, c_i * 512:(c_i + 1) * 512],
                        mpassall[:, c_i * 512:(c_i + 1) * 512], -1.0, ALU.add, deps=(bb,))
            dd = kb.tsc("vector", mpassall[:, c_i * 512:(c_i + 1) * 512],
                        mpassall[:, c_i * 512:(c_i + 1) * 512], 3e38, ALU.mult,
                        deps=(cc,))
            mtoks.append(dd)
        mtoks = tuple(mtoks)

        pmk_bf = kb.sb([64, 256], BF16, "pmkbf")
        pmkl = kb.dma("gpsimd", pmk_bf[:], pmkT)
        pmv_bf = kb.sb([16, 16 * 64], BF16, "pmvbf")
        pmvl = [kb.dma("gpsimd", pmv_bf[0:16, h * 64:(h + 1) * 64],
                       pmv[h * 16:(h + 1) * 16, :]) for h in range(16)]
        pm_tok = (pmkl,) + tuple(pmvl)

        qT5 = kb.sb([64, 512], BF16)
        kT5 = kb.sb([64, 528], BF16)
        vch5 = kb.sb([128, 4 * 64], BF16, "vch5")
        probsT = kb.sb([128, 4 * 512], BF16, "probsT5")
        probpm = kb.sb([16, 512], BF16)
        zacc = kb.sb([128, 512], F32)
        zrow = kb.sb([1, 512], F32)
        rzsb = kb.sb([64, 512], F32)
        o5 = kb.sb([64, 512], BF16)
        onescolf = kb.sb([128, 1], F32)
        ones64f = kb.sb([1, 64], F32)
        of1 = kb.memset("vector", onescolf[:], 1.0)
        of2 = kb.memset("vector", ones64f[:], 1.0)
        scT = ctx.enter_context(nc.psum_tensor("scT5", [128, 2048], F32))
        scpm = pool[2]
        avps = pool[3]
        zps = pool[2]
        rzps = pool[2]
        rd5 = {"pb": None, "pm": last_cv, "av": None, "z": None, "rz": None, "ld": None}
        ofin = []
        for si in range(T // SEG):
            s0 = si * SEG
            for h in range(16):
                lq = kb.dma("sync", qT5[:], q_dram[h * 64:(h + 1) * 64, s0:s0 + SEG],
                            deps=qk_tok + (rd5["ld"],))
                lk = kb.dma("sync", kT5[:, 16:528], k_dram[h * 64:(h + 1) * 64, s0:s0 + SEG],
                            deps=qk_tok + (rd5["ld"],))
                lkp = kb.copy("gpsimd", kT5[:, 0:16], pmk_bf[:, h * 16:(h + 1) * 16],
                              deps=pm_tok + (rd5["ld"],))
                lv = [kb.dma("sync", vch5[:, c_i * 64:(c_i + 1) * 64],
                             vtok_dram[s0 + c_i * 128:s0 + (c_i + 1) * 128,
                                       h * 64:(h + 1) * 64],
                             deps=v_tok + (rd5["av"],)) for c_i in range(4)]
                ldeps = (lq, lk, lkp) + tuple(lv)
                mmsT = []
                for c_i in range(4):
                    mmsT.append(kb.mm(kT5[:, 16 + c_i * 128:16 + (c_i + 1) * 128], qT5[:],
                                      scT[:, c_i * 512:(c_i + 1) * 512], start=True,
                                      stop=True, deps=ldeps + (rd5["pb"],)))
                mpm = kb.mm(kT5[:, 0:16], qT5[:], scpm[0:16, :], start=True, stop=True,
                            deps=ldeps + (rd5["pm"],))
                mad = kb.tt("vector", scT[:], scT[:], mpassall[:], ALU.add,
                            deps=tuple(mmsT) + mtoks)
                e1 = kb.act(probsT[:], scT[:], AF.Exp, bias=negC_sb[:, h:h + 1],
                            deps=(mad, lC))
                rd5["pb"] = e1
                e2 = kb.act(probpm[:], scpm[0:16, :], AF.Exp, bias=negC_sb[0:16, h:h + 1],
                            deps=(mpm, lC))
                z1 = kb.tt("gpsimd", zacc[:], probsT[:, 0:512], probsT[:, 512:1024],
                           ALU.add, deps=(e1, rd5["z"]))
                z2 = kb.tt("gpsimd", zacc[:], zacc[:],
                           probsT[:, 1024:1536], ALU.add, deps=(z1,))
                z3 = kb.tt("gpsimd", zacc[:], zacc[:],
                           probsT[:, 1536:2048], ALU.add, deps=(z2,))
                z4 = kb.tt("gpsimd", zacc[0:16, :], zacc[0:16, :], probpm[:], ALU.add,
                           deps=(z3, e2))
                zm = kb.mm(onescolf[:], zacc[:], zps[0:1, :], start=True, stop=True,
                           deps=(z4, of1, rd5["z"]))
                l1 = kb.act(zrow[:], zps[0:1, :], AF.Ln, deps=(zm,))
                rd5["z"] = l1
                l2 = kb.act(zrow[:], zrow[:], AF.Exp, scale=-1.0, deps=(l1,))
                bz = kb.mm(ones64f[:], zrow[:], rzps[0:64, :], start=True, stop=True,
                           deps=(l2, of2, rd5["rz"]))
                cz = kb.copy("scalar", rzsb[:], rzps[0:64, :], deps=(bz,))
                rd5["rz"] = cz
                rd5["pm"] = cz
                avm = []
                a0 = kb.mm(pmv_bf[0:16, h * 64:(h + 1) * 64], probpm[:], avps[0:64, :],
                           start=True, stop=False, deps=(e2, lC) + ldeps + (rd5["av"],))
                last = a0
                for c_i in range(4):
                    last = kb.mm(vch5[:, c_i * 64:(c_i + 1) * 64],
                                 probsT[:, c_i * 512:(c_i + 1) * 512], avps[0:64, :],
                                 start=False, stop=(c_i == 3), deps=(e1, lv[c_i]))
                oe = kb.tt("vector", o5[:], avps[0:64, :], rzsb[:], ALU.mult,
                           deps=(last, cz, rd5.get("so")))
                rd5["av"] = oe
                so = kb.dma("scalar", o_dram[h * 64:(h + 1) * 64, s0:s0 + SEG], o5[:],
                            deps=(oe,))
                rd5["so"] = so
                rd5["ld"] = oe
                ofin.append(so)
        o_tok = (ofin[-1],)

        xres = kb.sb([128, 512], F32)
        war2 = {"o": None}

        def cb_out(mi, t0, ps_list, deps):
            lx = kb.dma("sync", xres[:], x[128 * mi:128 * (mi + 1), t0:t0 + 512],
                        deps=deps + (war2["o"],))
            o1 = kb.tt("vector", xres[:], xres[:], ps_list[0][:], ALU.add, deps=(lx,) + deps)
            st = kb.dma("scalar", xo[128 * mi:128 * (mi + 1), t0:t0 + 512], xres[:],
                        deps=(o1,))
            war2["o"] = st
            return st
        fin, _, _ = emit_proj(kb, [Wout], o_dram, cb_out, 1024, D, 0, T,
                              rhs_deps=o_tok, wbufs=wbq, psums=psA, ps_state=ps_state,
                              shared=shr, name="wo5")
        kb.finalize(fin)
    return nc


def build_s3():
    HH, DHH = 16, 128
    DE = 2048
    nc = bass.Bass("TRN2", target_bir_lowering=False)
    x = nc.dram_tensor("x", [D, T], F32, kind="ExternalInput").ap()
    WinT = nc.dram_tensor("WinT", [D, 8208], F32, kind="ExternalInput").ap()
    WoutT = nc.dram_tensor("WoutT", [DE, D], F32, kind="ExternalInput").ap()
    b_q = nc.dram_tensor("b_q", [128, 16], F32, kind="ExternalInput").ap()
    b_k = nc.dram_tensor("b_k", [128, 16], F32, kind="ExternalInput").ap()
    b_v = nc.dram_tensor("b_v", [1, DE], F32, kind="ExternalInput").ap()
    b_p = nc.dram_tensor("b_p", [128, 16], F32, kind="ExternalInput").ap()
    b_y = nc.dram_tensor("b_y", [16, 1], F32, kind="ExternalInput").ap()
    lsc = nc.dram_tensor("lsc", [16, 1], F32, kind="ExternalInput").ap()
    smf = nc.dram_tensor("smf", [16, 1], F32, kind="ExternalInput").ap()
    iln_w = nc.dram_tensor("iln_w", [128, 8], F32, kind="ExternalInput").ap()
    iln_b = nc.dram_tensor("iln_b", [128, 8], F32, kind="ExternalInput").ap()
    oln_w = nc.dram_tensor("oln_w", [128, 8], F32, kind="ExternalInput").ap()
    oln_b = nc.dram_tensor("oln_b", [128, 8], F32, kind="ExternalInput").ap()
    xo = nc.dram_tensor("xo", [D, T], F32, kind="ExternalOutput").ap()

    xn0_dram = nc.dram_tensor("xn0_scr", [D, T], BF16, kind="Internal").ap()
    xn_dram = nc.dram_tensor("xn_scr", [D, T], BF16, kind="Internal").ap()
    q_dram = nc.dram_tensor("q_scr", [DE, T], BF16, kind="Internal").ap()
    k_dram = nc.dram_tensor("k_scr", [DE, T], BF16, kind="Internal").ap()
    vtok_dram = nc.dram_tensor("vtok_scr", [T, DE], BF16, kind="Internal").ap()
    p_dram = nc.dram_tensor("p_scr", [DE, T], BF16, kind="Internal").ap()
    pos_dram = nc.dram_tensor("pos_scr", [16, T], F32, kind="Internal").ap()
    ks_dram = nc.dram_tensor("ks_scr", [DE, T], BF16, kind="Internal").ap()
    o_dram = nc.dram_tensor("o_scr", [DE, T], BF16, kind="Internal").ap()
    acc_dram = nc.dram_tensor("acc_scr", [D, T], F32, kind="Internal").ap()
    lno_dram = nc.dram_tensor("lno_scr", [D, T], BF16, kind="Internal").ap()
    sc2_dram = nc.dram_tensor("sc2_scr", [1, 16], F32, kind="Internal").ap()
    negC_dram = nc.dram_tensor("negC_scr", [1, 16], F32, kind="Internal").ap()

    with contextlib.ExitStack() as ctx:
        kb = KB(nc, ctx)
        cn = Consts(kb)
        pool = [kb.ps([128, 512], F32, f"pool{i}") for i in range(4)]
        scT = ctx.enter_context(nc.psum_tensor("scT3", [128, 2048], F32))

        ilw = kb.sb([128, 8 * 4], F32, "ilnv")
        il1 = kb.dma("sync", ilw[:, 0:8], iln_w)
        il2 = kb.dma("sync", ilw[:, 8:16], iln_b)
        il3 = kb.dma("sync", ilw[:, 16:24], oln_w)
        il4 = kb.dma("sync", ilw[:, 24:32], oln_b)
        ln0 = emit_ln(kb, cn, x, xn0_dram, 0, T, (pool[0], pool[1]))
        ln1 = emit_ln(kb, cn, xn0_dram, xn_dram, 0, T, (pool[0], pool[1]),
                      w_ap=lambda i: ilw[:, i:i + 1], b_ap=lambda i: ilw[:, 8 + i:8 + i + 1],
                      wb_deps=(il1, il2), x_deps=(ln0[-1],), psum_deps=(ln0[-1],))
        ln_tok = (ln1[-1],)

        ls_sb = kb.sb([16, 2], F32)
        lsl = kb.dma("sync", ls_sb[0:16, 0:1], lsc)
        s2c = kb.act(ls_sb[0:16, 1:2], ls_sb[0:16, 0:1], AF.Exp, scale=-2.0, deps=(lsl,))
        s2d = kb.tsc("vector", ls_sb[0:16, 1:2], ls_sb[0:16, 1:2], 1.0 / (128.0 ** 0.5),
                     ALU.mult, deps=(s2c,))
        ssc = kb.dma("scalar", sc2_dram.rearrange("a h -> h a"), ls_sb[0:16, 1:2],
                     deps=(s2d,))
        sclb = kb.sb([128, 16], F32, "sclb")
        lscl = kb.dma("sync", sclb[:], sc2_dram.to_broadcast([128, 16]), deps=(ssc,))
        bqs = kb.sb([128, 16 * 3], F32, "bqs")
        bl1 = kb.dma("sync", bqs[:, 0:16], b_q)
        bl2 = kb.dma("sync", bqs[:, 16:32], b_k)
        bl3 = kb.dma("sync", bqs[:, 32:48], b_p)

        war = {"e": None}

        def mk_cb(dst, kind):
            def cb(mi, t0, ps_list, deps):
                if kind == "q":
                    o1 = kb.tsc("vector", qkp_sb[:], ps_list[0][:], bqs[:, mi:mi + 1],
                                ALU.add, sclb[:, mi:mi + 1], ALU.mult,
                                deps=deps + (bl1, lscl, war["e"]))
                elif kind == "k":
                    o1 = kb.tsc("vector", qkp_sb[:], ps_list[0][:], bqs[:, 16 + mi:16 + mi + 1],
                                ALU.add, deps=deps + (bl2, war["e"]))
                else:
                    o1 = kb.act(qkp_sb[:], ps_list[0][:], AF.Silu,
                                bias=bqs[:, 32 + mi:32 + mi + 1], deps=deps + (bl3, war["e"]))
                st = kb.dma("scalar", dst[128 * mi:128 * (mi + 1), t0:t0 + 512], qkp_sb[:],
                            deps=(o1,))
                war["e"] = st
                return st
            return cb

        qkp_sb = kb.sb([128, 512], BF16)
        shr = {}
        ps_state = {id(pool[0]): ln_tok[0], id(pool[1]): ln_tok[0]}
        psA = [[pool[0], pool[1]]]
        wb3 = [None]
        _, wl, _ = emit_proj(kb, [WinT[:, 0:2048]], xn_dram, mk_cb(q_dram, "q"), D, 2048,
                             0, T, rhs_deps=ln_tok, wbufs=wb3, psums=psA,
                             ps_state=ps_state, shared=shr, name="q3")
        wb3[0] = wl[0]
        _, wl, _ = emit_proj(kb, [WinT[:, 2048:4096]], xn_dram, mk_cb(k_dram, "k"), D, 2048,
                             0, T, rhs_deps=ln_tok, wbufs=wb3, psums=psA,
                             ps_state=ps_state, shared=shr, name="k3")
        wb3[0] = wl[0]
        _, wl, _ = emit_proj(kb, [WinT[:, 6144:8192]], xn_dram, mk_cb(p_dram, "p"), D, 2048,
                             0, T, rhs_deps=ln_tok, wbufs=wb3, psums=psA,
                             ps_state=ps_state, shared=shr, name="p3")
        wb3[0] = wl[0]
        qk_tok = (war["e"],)

        bvb = [kb.sb([128, 512], F32, f"bvb{j}") for j in range(4)]
        bvl = [kb.dma("sync", bvb[j][:], b_v[0:1, j * 512:(j + 1) * 512].to_broadcast([128, 512]))
               for j in range(4)]
        xs8 = kb.sb([128, 8 * 128], BF16, "xs8q")
        vsb = kb.sb([128, 512], BF16)
        vw, vw_last = wb3[0]
        vwl = [kb.dma("gpsimd", vw[:, kk * DE:(kk + 1) * DE],
                      WinT[128 * kk:(kk + 1) * 128, 4096:6144], deps=(vw_last,))
               for kk in range(8)]
        prev_v = None
        vfin = []
        for ti in range(T // 128):
            lxs = [kb.dma("sync", xs8[:, kk * 128:(kk + 1) * 128],
                          xn_dram[kk * 128:(kk + 1) * 128, ti * 128:(ti + 1) * 128],
                          deps=((prev_v,) if prev_v else ()) + ln_tok) for kk in range(8)]
            for nj in range(4):
                last = None
                for kk in range(8):
                    last = kb.mm(xs8[:, kk * 128:(kk + 1) * 128],
                                 vw[:, kk * DE + nj * 512:kk * DE + nj * 512 + 512],
                                 pool[2][:], start=(kk == 0), stop=(kk == 7),
                                 deps=(lxs[kk], vwl[kk]) + ((prev_v,) if kk == 0 else ()))
                cv = kb.tt("vector", vsb[:], pool[2][:], bvb[nj][:], ALU.add,
                           deps=(last, bvl[nj]))
                last_cv = cv
                sv = kb.dma("scalar", vtok_dram[ti * 128:(ti + 1) * 128, nj * 512:nj * 512 + 512],
                            vsb[:], deps=(cv,))
                prev_v = sv
                vfin.append(sv)
        v_tok = (vfin[-1],)
        wb3[0] = (vw, last)

        yw = kb.sb([128, 8 * 16], BF16, "yw3")
        ywl = [kb.dma("gpsimd", yw[:, kk * 16:(kk + 1) * 16],
                      WinT[128 * kk:(kk + 1) * 128, 8192:8208]) for kk in range(8)]
        by_sb = kb.sb([16, 1], F32)
        byl = kb.dma("sync", by_sb[0:16, :], b_y)
        ycar = kb.sb([16, 1], F32)
        yc0 = kb.memset("vector", ycar[:], 0.0)
        ninf3 = kb.sb([16, 1], F32)
        nf3 = kb.memset("vector", ninf3[:], -3e38)
        ysb = kb.sb([16, 512], F32)
        xbq = [kb.sb([128, 512], BF16) for _ in range(8)]
        prev_rd = ()
        prev_pos = None
        ycarry_tok = (yc0,)
        for t0 in range(0, T, 512):
            lds = [kb.dma("sync", xbq[kk][:], xn_dram[128 * kk:(kk + 1) * 128, t0:t0 + 512],
                          deps=prev_rd + ln_tok) for kk in range(8)]
            last = None
            for kk in range(8):
                last = kb.mm(yw[:, kk * 16:(kk + 1) * 16], xbq[kk][:], pool[3][0:16, :],
                             start=(kk == 0), stop=(kk == 7),
                             deps=(ywl[kk], lds[kk]) + (prev_rd if kk == 0 else ()))
            sg = kb.act(ysb[0:16, :], pool[3][0:16, :], AF.Sigmoid, bias=by_sb[0:16, 0:1],
                        deps=(last, byl, prev_pos))
            prev_rd = (sg,)
            sc = kb.scan(ysb[0:16, :], ysb[0:16, :], ninf3[0:16, 0:1].to_broadcast([16, 512]),
                         ycar[0:16, 0:1], ALU.add, ALU.max,
                         deps=(sg, nf3) + ycarry_tok)
            cu = kb.copy("vector", ycar[:], ysb[0:16, 511:512], deps=(sc,))
            ycarry_tok = (cu,)
            ps1 = kb.dma("scalar", pos_dram[0:16, t0:t0 + 512], ysb[0:16, :], deps=(sc, cu))
            prev_pos = ps1
        pos_tok = (prev_pos,)

        smr = kb.sb([16, 2], F32)
        sml = kb.dma("sync", smr[0:16, 0:1], smf)
        sm1 = kb.act(smr[0:16, 1:2], smr[0:16, 0:1], AF.Sigmoid, deps=(sml,))
        smd = nc.dram_tensor("smd_scr", [1, 16], F32, kind="Internal").ap()
        sms = kb.dma("scalar", smd.rearrange("a h -> h a"), smr[0:16, 1:2], deps=(sm1,))
        smb = kb.sb([128, 16], F32, "smb")
        sml2 = kb.dma("sync", smb[:], smd.to_broadcast([128, 16]), deps=(sms,))
        ka = kb.sb([128, 2048], BF16)
        kbf = kb.sb([128, 2048], BF16)
        ksm_prev = None
        for ci in range(16):
            for t0 in range(0, T, 2048):
                l1 = kb.dma("sync", ka[:], k_dram[ci * 128:(ci + 1) * 128, t0:t0 + 2048],
                            deps=qk_tok + ((ksm_prev,) if ksm_prev else ()))
                if t0 == 0:
                    zb = kb.memset("vector", kbf[:, 0:1], 0.0, deps=((ksm_prev,) if ksm_prev else ()))
                    l2 = kb.dma("sync", kbf[:, 1:2048],
                                k_dram[ci * 128:(ci + 1) * 128, 0:2047],
                                deps=qk_tok + ((ksm_prev,) if ksm_prev else ()))
                    l2d = (l2, zb)
                else:
                    l2 = kb.dma("sync", kbf[:],
                                k_dram[ci * 128:(ci + 1) * 128, t0 - 1:t0 + 2047],
                                deps=qk_tok + ((ksm_prev,) if ksm_prev else ()))
                    l2d = (l2,)
                d1 = kb.tt("vector", kbf[:], kbf[:], ka[:], ALU.subtract, deps=(l1,) + l2d)
                d2 = kb.stt("vector", ka[:], kbf[:], smb[:, ci:ci + 1], ka[:],
                            ALU.mult, ALU.add, deps=(d1, sml2))
                st = kb.dma("scalar", ks_dram[ci * 128:(ci + 1) * 128, t0:t0 + 2048], ka[:],
                            deps=(d2,))
                ksm_prev = st
        ksm_tok = (ksm_prev,)

        negC_row = kb.sb([1, 16], F32, "negCrow")
        negC_sb = kb.sb([128, 16], F32, "negCsb")
        sqt = kb.sb([128, 2048], BF16)
        hmax = kb.sb([1, 8], F32)
        lnq = kb.sb([1, 16], F32)
        lnk = kb.sb([1, 16], F32)
        onescolb = kb.sb([128, 1], BF16)
        oc1 = kb.memset("vector", onescolb[:], 1.0)
        rdC = {"sq": None, "nr": prev_rd[0]}
        lastmv = None
        for h in range(16):
            for which, dram, dstrow in ((0, q_dram, lnq), (1, ks_dram, lnk)):
                mv = None
                for ci4 in range(4):
                    t0 = ci4 * 2048
                    lqh = kb.dma("sync", sqt[:], dram[h * 128:(h + 1) * 128, t0:t0 + 2048],
                                 deps=qk_tok + ksm_tok + (rdC["sq"],))
                    sq1 = kb.act(sqt[:], sqt[:], AF.Square, deps=(lqh,))
                    subs = []
                    for jj in range(4):
                        nmj = kb.mm(onescolb[:], sqt[:, jj * 512:(jj + 1) * 512],
                                    pool[3][0:1, :], start=True, stop=True,
                                    deps=(sq1, oc1, rdC["nr"]))
                        rmj = kb.red(hmax[0:1, jj:jj + 1], pool[3][0:1, :], ALU.max,
                                     deps=(nmj,))
                        rdC["nr"] = rmj
                        subs.append(rmj)
                    rdC["sq"] = nmj
                    rm2 = kb.red(hmax[0:1, 4:5], hmax[0:1, 0:4], ALU.max, deps=tuple(subs))
                    if ci4 == 0:
                        mv = kb.copy("vector", dstrow[0:1, h:h + 1], hmax[0:1, 4:5],
                                     deps=(rm2, lastmv))
                    else:
                        mv = kb.tt("vector", dstrow[0:1, h:h + 1], dstrow[0:1, h:h + 1],
                                   hmax[0:1, 4:5], ALU.max, deps=(rm2, mv))
                lastmv = mv
        lv1 = kb.act(lnq[0:1, :], lnq[0:1, :], AF.Ln, deps=(lastmv,))
        lv2 = kb.act(lnk[0:1, :], lnk[0:1, :], AF.Ln, deps=(lastmv, lv1))
        cs = kb.tt("vector", negC_row[0:1, :], lnq[0:1, :], lnk[0:1, :], ALU.add,
                   deps=(lv1, lv2))
        ex2 = kb.act(negC_row[0:1, :], negC_row[0:1, :], AF.Exp, scale=0.5, deps=(cs,))
        ng2 = kb.tsc("vector", negC_row[0:1, :], negC_row[0:1, :], -1.0, ALU.mult,
                     deps=(ex2,))
        sC = kb.dma("scalar", negC_dram, negC_row[0:1, :], deps=(ng2,))
        lC = kb.dma("sync", negC_sb[:], negC_dram.to_broadcast([128, 16]), deps=(sC,))

        idf3 = kb.sb([128, 128], F32, "idf3")
        idl3 = kb.copy("vector", idf3[:], cn.ident_bf[:], deps=(cn.tok[0],))

        mpassall = kb.sb([128, 2048], F32, "mpassall3")
        mtoks = []
        for c_i in range(4):
            aa = kb.iota(mpassall[:, c_i * 512:(c_i + 1) * 512], [[1, 512]],
                         base=-c_i * 128, chmul=-1)
            bb = kb.tsc("vector", mpassall[:, c_i * 512:(c_i + 1) * 512],
                        mpassall[:, c_i * 512:(c_i + 1) * 512], -0.5, ALU.is_ge, deps=(aa,))
            cc = kb.tsc("vector", mpassall[:, c_i * 512:(c_i + 1) * 512],
                        mpassall[:, c_i * 512:(c_i + 1) * 512], -1.0, ALU.add, deps=(bb,))
            dd = kb.tsc("vector", mpassall[:, c_i * 512:(c_i + 1) * 512],
                        mpassall[:, c_i * 512:(c_i + 1) * 512], 3e38, ALU.mult, deps=(cc,))
            mtoks.append(dd)
        mtoks = tuple(mtoks)

        qT3 = kb.sb([128, 512], BF16)
        kT3 = kb.sb([128, 2048], BF16, "kT3")
        vch3 = kb.sb([128, 4 * 128], BF16, "vch3")
        probsT3 = kb.sb([128, 2048], BF16, "probsT3")
        poscol = kb.sb([128, 64], F32, "poscol")
        shrow = kb.sb([1, 512], F32)
        shrow_b = kb.sb([1, 512], BF16)
        onesrow_b = kb.sb([1, 128], BF16)
        ob1 = kb.memset("vector", onesrow_b[:], 1.0)
        zacc3 = kb.sb([128, 512], F32)
        zrow3 = kb.sb([1, 512], F32)
        rzsb3 = kb.sb([128, 512], F32)
        ptile = kb.sb([128, 512], BF16)
        o3 = kb.sb([128, 512], BF16)
        onescolf = kb.sb([128, 1], F32)
        of1 = kb.memset("vector", onescolf[:], 1.0)
        onesrowf = kb.sb([1, 128], F32)
        of2 = kb.memset("vector", onesrowf[:], 1.0)
        avps = pool[2]
        zps = pool[3]
        rzps = pool[3]
        rd3 = {"pb": None, "av": last_cv, "z": rdC["nr"], "rz": None, "sh": None,
               "so": None, "ld": None, "zg": None}
        ofin = []
        posrowt = kb.sb([64, 128], F32, "posrowt")
        for h in range(16):
            lpr = kb.dma("sync", posrowt[:],
                         pos_dram[h:h + 1, :].rearrange("a (c t) -> (a c) t", t=128),
                         deps=pos_tok + (rd3["ld"], rd3.get("lpr")))
            tpp = kb.transp(pool[3][0:128, 0:64], posrowt[:], idf3[0:64, 0:64],
                            deps=(lpr, idl3, rd3["z"]))
            lpos = kb.copy("vector", poscol[:], pool[3][0:128, 0:64], deps=(tpp,))
            rd3["z"] = lpos
            rd3["lpr"] = tpp
            for qt in range(16):
                q0 = qt * 512
                lq = kb.dma("sync", qT3[:], q_dram[h * 128:(h + 1) * 128, q0:q0 + 512],
                            deps=qk_tok + (rd3["ld"],))
                lsh = kb.dma("sync", shrow[0:1, :], pos_dram[h:h + 1, q0:q0 + 512],
                             deps=pos_tok + (rd3["sh"],))
                sh1 = kb.tsc("vector", shrow[0:1, :], shrow[0:1, :],
                             negC_row[0:1, h:h + 1], ALU.subtract, deps=(lsh, ng2))
                sh2 = kb.tsc("vector", shrow[0:1, :], shrow[0:1, :], -1.0, ALU.mult,
                             deps=(sh1,))
                sh3 = kb.copy("vector", shrow_b[0:1, :], shrow[0:1, :], deps=(sh2,))
                rd3["sh"] = sh3
                nchunks = (qt + 1) * 4
                avdone = None
                zgdone = rd3["zg"]
                for g0 in range(0, nchunks, 4):
                    gsz = min(4, nchunks - g0)
                    lk3 = kb.dma("sync", kT3[:, 0:gsz * 512].rearrange("p (a b) -> p (a b)") if False else kT3[:, 0:512 * gsz],
                                 k_dram[h * 128:(h + 1) * 128, g0 * 128:(g0 + gsz) * 128],
                                 deps=ksm_tok + (rd3["pb"],)) if False else None
                    lk3 = kb.dma("sync", kT3[:, 0:gsz * 128],
                                 ks_dram[h * 128:(h + 1) * 128, g0 * 128:(g0 + gsz) * 128],
                                 deps=ksm_tok + (rd3["pb"],))
                    lv3 = kb.dma("sync", vch3[:, 0:gsz * 128].rearrange("p (a d) -> p a d", d=128),
                                 vtok_dram[g0 * 128:(g0 + gsz) * 128,
                                           h * 128:(h + 1) * 128].rearrange(
                                               "(a p) d -> p a d", p=128),
                                 deps=v_tok + (rd3["av"], avdone))
                    mms = []
                    for ci in range(gsz):
                        kc = g0 + ci
                        m1 = kb.mm(kT3[:, ci * 128:(ci + 1) * 128], qT3[:],
                                   scT[:, ci * 512:(ci + 1) * 512], start=True, stop=False,
                                   deps=(lk3, lq, rd3["pb"]))
                        m2 = kb.mm(onesrow_b[0:1, :], shrow_b[0:1, :],
                                   scT[:, ci * 512:(ci + 1) * 512], start=False, stop=True,
                                   deps=(sh3, ob1, m1))
                        mms.append(m2)
                        if qt * 4 <= kc < qt * 4 + 4:
                            dmk = kb.tt("vector", scT[:, ci * 512:(ci + 1) * 512],
                                        scT[:, ci * 512:(ci + 1) * 512],
                                        mpassall[:, (kc - qt * 4) * 512:(kc - qt * 4 + 1) * 512],
                                        ALU.add, deps=(m2,) + mtoks)
                            mms.append(dmk)
                    exps = []
                    for ci in range(gsz):
                        kc = g0 + ci
                        e1 = kb.act(probsT3[:, ci * 512:(ci + 1) * 512],
                                    scT[:, ci * 512:(ci + 1) * 512], AF.Exp,
                                    bias=poscol[:, kc:kc + 1],
                                    deps=tuple(mms) + (lpos, lC, zgdone, avdone))
                        exps.append(e1)
                    rd3["pb"] = exps[-1]
                    for ci in range(gsz):
                        if g0 + ci == 0:
                            zz = kb.copy("gpsimd", zacc3[:], probsT3[:, 0:512],
                                         deps=(exps[ci], zgdone))
                        else:
                            zz = kb.tt("gpsimd", zacc3[:], zacc3[:],
                                       probsT3[:, ci * 512:(ci + 1) * 512],
                                       ALU.add, deps=(exps[ci], zgdone))
                        zgdone = zz
                    for ci in range(gsz):
                        avdone = kb.mm(vch3[:, ci * 128:(ci + 1) * 128],
                                       probsT3[:, ci * 512:(ci + 1) * 512],
                                       avps[:], start=(g0 + ci == 0),
                                       stop=(g0 + ci == nchunks - 1),
                                       deps=(exps[ci], lv3) +
                                            ((rd3["av"],) if g0 + ci == 0 else ()))
                rd3["zg"] = zgdone
                zm = kb.mm(onescolf[:], zacc3[:], zps[0:1, :], start=True, stop=True,
                           deps=(zgdone, of1, rd3["z"]))
                l1 = kb.act(zrow3[:], zps[0:1, :], AF.Ln, deps=(zm,))
                rd3["z"] = l1
                l2 = kb.act(zrow3[:], zrow3[:], AF.Exp, scale=-1.0, deps=(l1,))
                bz = kb.mm(onesrowf[0:1, :], zrow3[:], rzps[:], start=True, stop=True,
                           deps=(l2, of2, rd3["rz"], l1))
                cz = kb.copy("scalar", rzsb3[:], rzps[:], deps=(bz,))
                rd3["rz"] = cz
                lp = kb.dma("sync", ptile[:], p_dram[h * 128:(h + 1) * 128, q0:q0 + 512],
                            deps=qk_tok + (rd3.get("po"),))
                oe1 = kb.tt("vector", o3[:], avps[:], rzsb3[:], ALU.mult,
                            deps=(avdone, cz, rd3["so"]))
                oe2 = kb.tt("vector", o3[:], o3[:], ptile[:], ALU.mult, deps=(oe1, lp))
                rd3["po"] = oe2
                so = kb.dma("scalar", o_dram[h * 128:(h + 1) * 128, q0:q0 + 512], o3[:],
                            deps=(oe2,))
                rd3["so"] = so
                rd3["av"] = oe1
                rd3["ld"] = oe2
                ofin.append(so)
        o_tok = (ofin[-1],)

        acc_sb = kb.sb([128, 512], F32)
        war4 = {"a": None}

        def cb_acc(mi, t0, ps_list, deps):
            o1 = kb.copy("vector", acc_sb[:], ps_list[0][:], deps=deps + (war4["a"],))
            st = kb.dma("scalar", acc_dram[128 * mi:128 * (mi + 1), t0:t0 + 512], acc_sb[:],
                        deps=(o1,))
            war4["a"] = st
            return st
        accf, _, _ = emit_proj(kb, [WoutT], o_dram, cb_acc, DE, D, 0, T,
                               rhs_deps=o_tok, wbufs=wb3, psums=psA, ps_state=ps_state,
                               shared=shr, name="wo3")
        lnf = emit_ln(kb, cn, acc_dram, lno_dram, 0, T, (pool[0], pool[1]),
                      w_ap=lambda i: ilw[:, 16 + i:16 + i + 1],
                      b_ap=lambda i: ilw[:, 24 + i:24 + i + 1],
                      wb_deps=(il3, il4), x_deps=(accf[-1],), psum_deps=(accf[-1],))
        fin = emit_residual(kb, x, lno_dram, xo, (lnf[-1],))
        kb.finalize(fin)
    return nc


def build_s6_dense():
    E8, II = 8, 2048
    nc = bass.Bass("TRN2", target_bir_lowering=False)
    x = nc.dram_tensor("x", [D, T], F32, kind="ExternalInput").ap()
    gw = nc.dram_tensor("gw", [E8, D, II], F32, kind="ExternalInput").ap()
    uw = nc.dram_tensor("uw", [E8, D, II], F32, kind="ExternalInput").ap()
    dw = nc.dram_tensor("dw", [E8, II, D], F32, kind="ExternalInput").ap()
    shg = nc.dram_tensor("shg", [D, II], F32, kind="ExternalInput").ap()
    shu = nc.dram_tensor("shu", [D, II], F32, kind="ExternalInput").ap()
    shd = nc.dram_tensor("shd", [II, D], F32, kind="ExternalInput").ap()
    rw = nc.dram_tensor("rw", [D, 8], F32, kind="ExternalInput").ap()
    lnw = nc.dram_tensor("lnw", [128, 8], F32, kind="ExternalInput").ap()
    lnb = nc.dram_tensor("lnb", [128, 8], F32, kind="ExternalInput").ap()
    xo = nc.dram_tensor("xo", [D, T], F32, kind="ExternalOutput").ap()

    xn0_dram = nc.dram_tensor("xn0_scr", [D, T], BF16, kind="Internal").ap()
    xn_dram = nc.dram_tensor("xn_scr", [D, T], BF16, kind="Internal").ap()
    xnf_dram = nc.dram_tensor("xnf_scr", [D, T], F32, kind="Internal").ap()
    hh_dram = nc.dram_tensor("hh_scr", [II, T], BF16, kind="Internal").ap()
    acc_dram = nc.dram_tensor("acc_scr", [D, T], F32, kind="Internal").ap()
    ge8_dram = nc.dram_tensor("ge8_scr", [8, T], F32, kind="Internal").ap()

    with contextlib.ExitStack() as ctx:
        kb = KB(nc, ctx)
        cn = Consts(kb)
        pool = [kb.ps([128, 512], F32, f"pool{i}") for i in range(8)]

        lw1 = kb.dma("sync", lnwv[:, 0:8] if False else None, None) if False else None
        lnwv = kb.sb([128, 16], F32, "lnwv")
        lw1 = kb.dma("sync", lnwv[:, 0:8], lnw)
        lw2 = kb.dma("sync", lnwv[:, 8:16], lnb)
        ln0 = emit_ln(kb, cn, x, xn0_dram, 0, T, (pool[6], pool[7]))
        ln1 = emit_ln(kb, cn, xn0_dram, xn_dram, 0, T, (pool[6], pool[7]),
                      w_ap=lambda i: lnwv[:, i:i + 1], b_ap=lambda i: lnwv[:, 8 + i:8 + i + 1],
                      wb_deps=(lw1, lw2), x_deps=(ln0[-1],), psum_deps=(ln0[-1],),
                      xn32_dram=xnf_dram)
        ln_tok = (ln1[-1],)

        idf6 = kb.sb([128, 128], F32, "idf6")
        idl6 = kb.copy("vector", idf6[:], cn.ident_bf[:], deps=(cn.tok[0],))
        ge_last = emit_l1_router(kb, cn, nc, xnf_dram, rw, ge8_dram, ln_tok,
                                 pool[6], pool[7], idf6, idl6)

        h1sb = kb.sb([128, 512], BF16)
        hhsb = kb.sb([128, 512], BF16)
        ysb = kb.sb([128, 512], F32)
        gebt = kb.sb([128, 512], F32)
        wb13 = [None, None]
        wb2 = [None]
        ps13 = [[pool[0], pool[1]], [pool[2], pool[3]]]
        ps2 = [[pool[4], pool[5]]]
        ps_state = {id(pool[0]): ln_tok[0], id(pool[1]): ln_tok[0]}
        shr = {}
        war = {"h1_rd": None, "hh_st": None, "ge_rd": None}
        prev_hh_store = {}
        prev_acc = [None]

        xe_dram = nc.dram_tensor("xe_scr", [D, T], BF16, kind="Internal").ap()
        xesb = kb.sb([128, 512], BF16)
        war_xe = {"st": None, "ge": None}

        def emit_xe(e):
            last = None
            for mi in range(8):
                for t0 in range(0, T, 512):
                    l1 = kb.dma("sync", xesb[:], xn_dram[128 * mi:128 * (mi + 1), t0:t0 + 512],
                                deps=ln_tok + (war_xe["st"],))
                    lg2 = kb.dma("gpsimd", gebt[:],
                                 ge8_dram[e:e + 1, t0:t0 + 512].to_broadcast([128, 512]),
                                 deps=(ge_last, war_xe["ge"], war_xe["st"]))
                    o1 = kb.tt("vector", xesb[:], xesb[:], gebt[:], ALU.mult,
                               deps=(l1, lg2))
                    war_xe["ge"] = o1
                    st = kb.dma("scalar", xe_dram[128 * mi:128 * (mi + 1), t0:t0 + 512],
                                xesb[:], deps=(o1,))
                    war_xe["st"] = st
                    last = st
            return last

        def gated(e, first):
            if e >= 0:
                xe_tok = (emit_xe(e),)
                src = xe_dram
            else:
                xe_tok = ln_tok
                src = xn_dram

            def cb13(mi, t0, ps_list, deps):
                o1 = kb.act(h1sb[:], ps_list[0][:], AF.Silu, deps=deps + (war["h1_rd"],))
                o2 = kb.tt("vector", hhsb[:], ps_list[1][:], h1sb[:], ALU.mult,
                           deps=deps + (o1, war["hh_st"]))
                st = kb.dma("scalar", hh_dram[128 * mi:128 * (mi + 1), t0:t0 + 512], hhsb[:],
                            deps=(o2,))
                war["h1_rd"] = o2
                war["hh_st"] = st
                prev_hh_store[(mi, t0)] = st
                return st
            _, wl, _ = emit_proj(kb, [gw[e] if e >= 0 else shg, uw[e] if e >= 0 else shu],
                                 src, cb13, D, II, 0, T,
                                 rhs_deps=xe_tok, wbufs=wb13, psums=ps13,
                                 ps_state=ps_state, shared=shr, name="g6")
            wb13[0], wb13[1] = wl[0], wl[1]

            def cb2(mi, t0, ps_list, deps):
                o1 = kb.copy("vector", ysb[:], ps_list[0][:], deps=deps + (prev_acc[0],))
                st = kb.dma("gpsimd", acc_dram[128 * mi:128 * (mi + 1), t0:t0 + 512], ysb[:],
                            deps=(o1,), **({} if first else dict(accum_op=ALU.add)))
                prev_acc[0] = st
                return st
            _, wl2, _ = emit_proj(kb, [dw[e] if e >= 0 else shd], hh_dram, cb2, II, D, 0, T,
                                  rhs_deps=(prev_hh_store[(II // 128 - 1, T - 512)],),
                                  wbufs=wb2, psums=ps2, ps_state=ps_state, shared=shr,
                                  name="d6")
            wb2[0] = wl2[0]

        for e in range(8):
            gated(e, first=(e == 0))
        gated(-1, first=False)

        xa = kb.sb([128, 512], F32)
        xc = kb.sb([128, 512], F32)
        xnl = kb.sb([128, 512], F32)
        fin = []
        prev = None
        for mi in range(8):
            for t0 in range(0, T, 512):
                l1 = kb.dma("sync", xa[:], x[128 * mi:128 * (mi + 1), t0:t0 + 512],
                            deps=(prev,))
                l2 = kb.dma("gpsimd", xc[:], acc_dram[128 * mi:128 * (mi + 1), t0:t0 + 512],
                            deps=(prev_acc[0], prev))
                l3 = kb.dma("gpsimd", xnl[:], xn0_dram[128 * mi:128 * (mi + 1), t0:t0 + 512],
                            deps=ln_tok + ((prev,) if prev else ()))
                o = kb.tt("vector", xa[:], xa[:], xc[:], ALU.add, deps=(l1, l2))
                o2 = kb.tt("vector", xa[:], xa[:], xnl[:], ALU.add, deps=(o, l3))
                st = kb.dma("scalar", xo[128 * mi:128 * (mi + 1), t0:t0 + 512], xa[:],
                            deps=(o2,))
                prev = st
                fin.append(st)
        kb.finalize(fin)
    return nc


def emit_l1_router(kb, cn, nc, xn_dram, rw, ge8_dram, ln_tok, bank_a, bank_b, idf, idl):
    gwsb = kb.sb([128, 8 * 8], F32)
    gwl = [kb.dma("gpsimd", gwsb[:, k * 8:(k + 1) * 8], rw[128 * k:(k + 1) * 128, :])
           for k in range(8)]
    lg_ps = bank_a
    lgT_ps = bank_b[:, 0:32]
    lgf = kb.sb([128, 512], F32)
    lgtm = kb.sb([128, 32], F32)
    gef = kb.sb([128, 32], F32)
    geT_ps = bank_a
    gefm = kb.sb([128, 512], F32)
    m1 = kb.sb([128, 1], F32)
    sc1 = kb.sb([128, 1], F32)
    msk = kb.sb([128, 8], F32)
    xbq = [kb.sb([128, 512], F32) for _ in range(8)]
    prev_mm = None
    prev_cycle = None
    last_store = None
    for t0 in range(0, T, 512):
        lds = [kb.dma("sync", xbq[k][:], xn_dram[128 * k:(k + 1) * 128, t0:t0 + 512],
                      deps=((prev_mm,) if prev_mm else ()) + tuple(ln_tok))
               for k in range(8)]
        last = None
        for k in range(8):
            last = kb.mm(gwsb[:, k * 8:(k + 1) * 8], xbq[k][:], lg_ps[0:8, :],
                         start=(k == 0), stop=(k == 7),
                         deps=(gwl[k], lds[k]) + ((prev_cycle,) if k == 0 else ()))
        prev_mm = last
        cp = kb.copy("vector", lgf[0:8, :], lg_ps[0:8, :], deps=(last,))
        tps = []
        prev_tp = None
        for j in range(4):
            tp = kb.transp(lgT_ps[:, j * 8:(j + 1) * 8],
                           lgf[0:8, j * 128:(j + 1) * 128],
                           idf[0:8, 0:8],
                           deps=(cp, idl, prev_tp) + ((prev_cycle,) if j == 0 else ()))
            tps.append(tp)
            prev_tp = tp
        cp2 = kb.copy("vector", lgtm[:], lgT_ps, deps=tuple(tps))
        ge_ready = []
        for j in range(4):
            lgj = lgtm[:, j * 8:(j + 1) * 8]
            r1 = kb.red(m1[:, 0:1], lgj, ALU.max, deps=(cp2,))
            e1 = kb.tsc("vector", msk[:, 0:8], lgj, m1[:, 0:1], ALU.is_equal, deps=(r1,))
            p1 = kb.act(sc1[:, 0:1], m1[:, 0:1], AF.Sigmoid, deps=(r1,))
            g1 = kb.tsc("vector", gef[:, j * 8:(j + 1) * 8], msk[:, 0:8], sc1[:, 0:1],
                        ALU.mult, deps=(e1, p1))
            ge_ready.append(g1)
        tps2 = []
        prev_tp = None
        for j in range(4):
            tp = kb.transp(geT_ps[0:8, j * 128:(j + 1) * 128],
                           gef[:, j * 8:(j + 1) * 8], idf[:],
                           deps=(ge_ready[j], idl, prev_tp) +
                                ((last_store,) if j == 0 else ()))
            tps2.append(tp)
            prev_tp = tp
        cp3 = kb.copy("vector", gefm[0:8, :], geT_ps[0:8, :], deps=tuple(tps2))
        last_store = kb.dma("scalar", ge8_dram[0:8, t0:t0 + 512], gefm[0:8, :], deps=(cp3,))
        prev_cycle = cp3
    return last_store


def _host_consts():
    DIM_HEAD, THETA = 64, 10000.0
    pos = np.arange(T)
    inv = 1.0 / THETA ** (np.arange(0, DIM_HEAD, 2) / DIM_HEAD)
    ang = np.repeat(pos[:, None] * inv, 2, axis=-1)
    cos = np.cos(ang).T.astype(np.float32)
    sin = np.sin(ang).T.astype(np.float32)
    cos2 = np.ascontiguousarray(np.vstack([cos, cos]))
    sin2 = np.ascontiguousarray(np.vstack([sin, sin]))
    R = np.zeros((128, 128), np.float32)
    for i in range(128):
        if i % 2 == 0:
            R[i + 1, i] = -1.0
        else:
            R[i - 1, i] = 1.0
    return cos2, sin2, R


def _col8(v, n=8):
    return np.ascontiguousarray(v.reshape(n, 128).T)


_BUILD_CACHE = {}


def _get(name, fn):
    if name not in _BUILD_CACHE:
        _BUILD_CACHE[name] = fn()
    return _BUILD_CACHE[name]


def kernel(**z):
    xs = [np.ascontiguousarray(z["L0_attn_a_x"][b].T) for b in range(B)]
    cid = list(range(8))
    cons = Consts.host_inputs()
    cos2, sin2, R = _host_consts()
    pm = z["L1_attn_a_pm"]
    pmkT = np.ascontiguousarray(np.concatenate([pm[0, h].T for h in range(16)], axis=1))
    pmv = np.ascontiguousarray(np.concatenate([pm[1, h] for h in range(16)], axis=0))
    b_in = z["L0_attn_b_b_in"]

    def launch(nc, per_core_extra):
        in_maps = []
        for c in range(8):
            m = dict(per_core_extra)
            m["x"] = xs[c // 2]
            m.update(cons)
            in_maps.append(m)
        res = run_bass_kernel_spmd(nc, in_maps, core_ids=cid)
        for b in range(B):
            xs[b] = res.results[2 * b]["xo"]

    launch(_get("s1", build_s1), {
        "Wkt": np.ascontiguousarray(z["L0_attn_a_Wk"].T),
        "Wvt": np.ascontiguousarray(z["L0_attn_a_Wv"].T),
        "Wrt": np.ascontiguousarray(z["L0_attn_a_Wr"].T),
        "Wot": np.ascontiguousarray(z["L0_attn_a_Wo"].T),
        "wv8": _col8(z["L0_attn_a_w"]), "uv8": _col8(z["L0_attn_a_u"]),
        "mk8": _col8(z["L0_attn_a_mix_k"]), "mv8": _col8(z["L0_attn_a_mix_v"]),
        "mr8": _col8(z["L0_attn_a_mix_r"]),
    })
    s2_inputs = {
        "ew1t": np.ascontiguousarray(z["L0_ffn_ew1"].transpose(0, 2, 1)),
        "ew2t": np.ascontiguousarray(z["L0_ffn_ew2"].transpose(0, 2, 1)),
        "ew3t": np.ascontiguousarray(z["L0_ffn_ew3"].transpose(0, 2, 1)),
        "fc1t": np.ascontiguousarray(z["L0_ffn_fc1"].T),
        "fc2t": np.ascontiguousarray(z["L0_ffn_fc2"].T),
        "fc3t": np.ascontiguousarray(z["L0_ffn_fc3"].T),
        "gate_wt": np.ascontiguousarray(z["L0_ffn_gate_w"].T),
        "gate_b": z["L0_ffn_gate_b"].reshape(1, 6),
    }
    launch(_get("s2", build_s2_dense), s2_inputs)
    launch(_get("s3", build_s3), {
        "WinT": np.ascontiguousarray(z["L0_attn_b_W_in"].T),
        "WoutT": np.ascontiguousarray(z["L0_attn_b_W_out"].T),
        "b_q": _col8(b_in[0:2048], 16), "b_k": _col8(b_in[2048:4096], 16),
        "b_v": b_in[4096:6144].reshape(1, 2048), "b_p": _col8(b_in[6144:8192], 16),
        "b_y": b_in[8192:8208].reshape(16, 1),
        "lsc": z["L0_attn_b_log_scale"].reshape(16, 1),
        "smf": z["L0_attn_b_smear_factor"].reshape(16, 1),
        "iln_w": _col8(z["L0_attn_b_in_ln_w"]), "iln_b": _col8(z["L0_attn_b_in_ln_b"]),
        "oln_w": _col8(z["L0_attn_b_out_ln_w"]), "oln_b": _col8(z["L0_attn_b_out_ln_b"]),
    })
    launch(_get("s2", build_s2_dense), s2_inputs)
    launch(_get("s5", build_s5), {
        "Wqkv": z["L1_attn_a_W_qkv"], "Wout": z["L1_attn_a_W_out"],
        "gn8": _col8(z["L1_attn_a_g_norm"]),
        "pmkT": pmkT, "pmv": pmv, "costab": cos2, "sintab": sin2, "rotm": R,
    })
    s6_inputs = {
        "gw": z["L1_ffn_gate_w"], "uw": z["L1_ffn_up_w"], "dw": z["L1_ffn_down_w"],
        "shg": z["L1_ffn_sh_gate"], "shu": z["L1_ffn_sh_up"], "shd": z["L1_ffn_sh_down"],
        "rw": z["L1_ffn_router_w"],
        "lnw": _col8(z["L1_ffn_ln_w"]), "lnb": _col8(z["L1_ffn_ln_b"]),
    }
    launch(_get("s6", build_s6_dense), s6_inputs)
    launch(_get("s7", build_s7), {
        "Wwt": np.ascontiguousarray(z["L1_attn_b_W_write"].T),
        "Wot": np.ascontiguousarray(z["L1_attn_b_W_out"].T),
        "Wgba": np.ascontiguousarray(np.concatenate(
            [z["L1_attn_b_W_gate"].T, z["L1_attn_b_W_beta"].T,
             z["L1_attn_b_W_alpha"].T], axis=1)),
        "Alog": z["L1_attn_b_A_log"].reshape(16, 1),
        "dtb": z["L1_attn_b_dt_bias"].reshape(16, 1),
    })
    launch(_get("s6", build_s6_dense), s6_inputs)

    out = np.stack([xs[b].T for b in range(B)], axis=0).astype(np.float32)
    return out
